# version 01
# baseline (speedup 1.0000x reference)
import numpy as np
import contextlib
from collections import defaultdict
import concourse.bass as bass
import concourse.mybir as mybir
from concourse.bass_utils import run_bass_kernel_spmd

F32 = mybir.dt.float32
BF16 = mybir.dt.bfloat16
AF = mybir.ActivationFunctionType
ALU = mybir.AluOpType
AX = mybir.AxisListType

NCORE = 8
B, L, D = 2, 4096, 2048
T = B * L
TPC = T // NCORE
DI, DS, NH, HD = 4096, 64, 64, 64
HPC = NH // NCORE
Q = 128
NCH = L // Q
ZC = DI // NCORE
E, FF, SF = 8, 1024, 1024
S128I = 1.0 / float(np.sqrt(128.0))
EPS_LN = 1e-5

DBG = []


class P:
    def __init__(self, nc, block, sems):
        self.nc, self.block = nc, block
        self.cnt = defaultdict(int)
        self.sem = sems

    def dma(self, eng, key, out, in_, **kw):
        if self.cnt[key] > 0:
            eng.wait_ge(self.sem[key], self.cnt[key])
        eng.dma_start(out=out, in_=in_, **kw).then_inc(self.sem[key], 16)
        self.cnt[key] += 16
        return self.cnt[key]

    def tick(self, instr, key, n=1):
        instr.then_inc(self.sem[key], n)
        self.cnt[key] += n
        return self.cnt[key]

    def wait(self, eng, key, val=None):
        v = self.cnt[key] if val is None else val
        if v > 0:
            eng.wait_ge(self.sem[key], v)


def build_nc():
    nc = bass.Bass("TRN2", target_bir_lowering=False)
    dt = nc.dram_tensor

    x0 = dt("x0", [TPC, D], F32, kind="ExternalInput").ap()
    wmam = dt("wmam", [D, 1160], F32, kind="ExternalInput").ap()
    convw = dt("convw", [128, 24], F32, kind="ExternalInput").ap()
    convb = dt("convb", [128, 6], F32, kind="ExternalInput").ap()
    dtbias = dt("dtbias", [8, 1], F32, kind="ExternalInput").ap()
    alog = dt("alog", [8, 1], F32, kind="ExternalInput").ap()
    dbc = dt("dbc", [128, 8], F32, kind="ExternalInput").ap()
    rmsw1 = dt("rmsw1", [128, D], F32, kind="ExternalInput").ap()
    normw = dt("normw", [128, DI], F32, kind="ExternalInput").ap()
    wop = dt("wop", [DI, D], F32, kind="ExternalInput").ap()
    ffnw = dt("ffnw", [128, D], F32, kind="ExternalInput").ap()
    routw = dt("routw", [D, E], F32, kind="ExternalInput").ap()
    wgate = dt("wgate", [E, D, FF], F32, kind="ExternalInput").ap()
    wup = dt("wup", [E, D, FF], F32, kind="ExternalInput").ap()
    wdown = dt("wdown", [E, FF, D], F32, kind="ExternalInput").ap()
    swg = dt("swg", [D, SF], F32, kind="ExternalInput").ap()
    swu = dt("swu", [D, SF], F32, kind="ExternalInput").ap()
    swd = dt("swd", [SF, D], F32, kind="ExternalInput").ap()
    wqkv = dt("wqkv", [3, D, 256], F32, kind="ExternalInput").ap()
    wqkvs = dt("wqkvs", [3, D, 256], F32, kind="ExternalInput").ap()
    bqk = dt("bqk", [128, 4], F32, kind="ExternalInput").ap()
    bvb = dt("bvb", [128, 256], F32, kind="ExternalInput").ap()
    woc = dt("woc", [D, D], F32, kind="ExternalInput").ap()
    wos = dt("wos", [D, D], F32, kind="ExternalInput").ap()
    bob = dt("bob", [128, D], F32, kind="ExternalInput").ap()
    identf = dt("identf", [128, 128], F32, kind="ExternalInput").ap()
    identb = dt("identb", [128, 128], BF16, kind="ExternalInput").ap()
    trim = dt("trim", [128, 128], F32, kind="ExternalInput").ap()
    cepsi = dt("cepsi", [128, 4], F32, kind="ExternalInput").ap()
    out = dt("o", [TPC, D], F32, kind="ExternalOutput").ap()

    u_sl = dt("u_sl", [TPC, D], BF16)
    u_full = dt("u_full", [T, D], BF16)
    xbcd = dt("xbcd", [6, 128, T], BF16)
    xbcc = dt("xbcc", [6, 128, T], BF16)
    siluz = dt("siluz", [T, ZC], BF16)
    ellh = dt("ellh", [8, T], F32)
    yg_sl = dt("yg_sl", [T, ZC], BF16)
    yg_full = dt("yg_full", [NCORE, T, ZC], BF16)
    yn_h = dt("yn_h", [TPC, DI], BF16)
    x1h = dt("x1h", [TPC, D], F32)
    hh = dt("hh", [TPC, D], BF16)
    x2h = dt("x2h", [TPC, D], F32)
    u2_sl = dt("u2_sl", [TPC, D], BF16)
    u2_full = dt("u2_full", [T, D], BF16)
    ao_sl = dt("ao_sl", [256, T], BF16)
    ao_full = dt("ao_full", [NCORE, 256, T], BF16)
    x3h = dt("x3h", [TPC, D], F32)
    ihbm = {"u_sl": u_sl, "siluz": siluz, "xbcc": xbcc, "ellh": ellh,
            "yg_sl": yg_sl, "yn_h": yn_h, "x1h": x1h, "hh": hh, "x2h": x2h,
            "u2_sl": u2_sl, "ao_sl": ao_sl, "x3h": x3h}
    dbg_outs = {}
    for name in DBG:
        src = ihbm[name]
        sh = list(src.shape) if not hasattr(src, "ap") else list(src.ap().shape)
        dbg_outs[name] = (dt("dbg_" + name, sh, src.ap().dtype,
                             kind="ExternalOutput").ap(), src)

    STK = contextlib.ExitStack()
    with STK:
        EC = STK.enter_context
        c_idf = EC(nc.sbuf_tensor("c_idf", [128, 128], F32))
        c_idb = EC(nc.sbuf_tensor("c_idb", [128, 128], BF16))
        c_tri = EC(nc.sbuf_tensor("c_tri", [128, 128], F32))
        c_onesb = EC(nc.sbuf_tensor("c_onesb", [128, 1], BF16))
        c_onesf = EC(nc.sbuf_tensor("c_onesf", [1, 128], F32))
        c_dbc = EC(nc.sbuf_tensor("c_dbc", [128, 8], F32))
        c_convw = EC(nc.sbuf_tensor("c_convw", [128, 24], F32))
        c_convb = EC(nc.sbuf_tensor("c_convb", [128, 6], F32))
        c_small = EC(nc.sbuf_tensor("c_small", [8, 8], F32))
        c_eps = EC(nc.sbuf_tensor("c_eps", [128, 4], F32))
        st_m = EC(nc.sbuf_tensor("st_m", [128, 8], F32))
        st_a = EC(nc.sbuf_tensor("st_a", [128, 8], F32))
        st_b = EC(nc.sbuf_tensor("st_b", [128, 8], F32))
        st_c = EC(nc.sbuf_tensor("st_c", [128, 8], F32))
        st_t = EC(nc.sbuf_tensor("st_t", [128, 8], F32))
        tp1a = EC(nc.sbuf_tensor("tp1a", [128, 1], F32))
        tp1b = EC(nc.sbuf_tensor("tp1b", [128, 1], F32))
        wtok = EC(nc.sbuf_tensor("wtok", [128, 64], F32))
        s_qs = EC(nc.semaphore("s_qs"))
        s_qg = EC(nc.semaphore("s_qg"))
        s_te = EC(nc.semaphore("s_te"))
        s_ve = EC(nc.semaphore("s_ve"))
        s_se = EC(nc.semaphore("s_se"))
        s_cc = EC(nc.semaphore("s_cc"))
        block = EC(nc.Block())
        p = P(nc, block, {"qs": s_qs, "qg": s_qg, "te": s_te, "ve": s_ve,
                          "se": s_se, "cc": s_cc})

        def SEC(name):
            def deco(f):
                if name in ("vector", "scalar"):
                    def wrapped(eng, _f=f):
                        eng.drain()
                        _f(eng)
                    getattr(block, name)(wrapped)
                else:
                    getattr(block, name)(f)
            return deco

        @SEC("gpsimd")
        def _(g):
            p.dma(g, "qg", c_idf[:, :], identf)
            p.dma(g, "qg", c_idb[:, :], identb)
            p.dma(g, "qg", c_tri[:, :], trim)
            p.dma(g, "qg", c_dbc[:, :], dbc)
            p.dma(g, "qg", c_convw[:, :], convw)
            p.dma(g, "qg", c_convb[:, :], convb)
            p.dma(g, "qg", c_small[0:8, 0:1], dtbias)
            p.dma(g, "qg", c_small[0:8, 1:2], alog)
            p.dma(g, "qg", c_eps[:, :], cepsi)
            p.tick(g.memset(c_onesb[:, :], 1.0), "cc")
            p.tick(g.memset(c_onesf[:, :], 1.0), "cc")
        consts_qg = p.cnt["qg"]

        @SEC("scalar")
        def _(s):
            p.wait(s, "qg", consts_qg)
            p.tick(s.activation(c_small[0:8, 3:4], c_small[0:8, 1:2], AF.Exp), "se")

        @SEC("vector")
        def _(v):
            p.wait(v, "se")
            p.tick(v.tensor_scalar_mul(c_small[0:8, 2:3], c_small[0:8, 3:4], -1.0), "ve")

        def emit_norm(x_hbm, w_bcast, eps2, dst_hbm, xa, scr, keep_stats):
            for tt in range(8):
                @SEC("gpsimd")
                def _(g, tt=tt):
                    p.wait(g, "ve")
                    p.dma(g, "qg", xa, x_hbm[tt * 128:(tt + 1) * 128, :])
                ld = p.cnt["qg"]

                @SEC("scalar")
                def _(s, tt=tt, ld=ld):
                    p.wait(s, "qg", ld)
                    p.wait(s, "qs")
                    p.tick(s.activation(scr, xa, AF.Square,
                                        accum_out=st_t[:, tt:tt + 1]), "se")
                sq = p.cnt["se"]

                @SEC("vector")
                def _(v, tt=tt, ld=ld, sq=sq):
                    p.wait(v, "qg", ld)
                    p.tick(v.reduce_sum(st_m[:, tt:tt + 1], xa, axis=AX.X), "ve")
                    v.drain()
                    p.tick(v.tensor_scalar_mul(st_m[:, tt:tt + 1],
                                               st_m[:, tt:tt + 1], 1.0 / D), "ve")
                    p.wait(v, "se", sq)
                    p.tick(v.tensor_scalar_mul(st_t[:, tt:tt + 1],
                                               st_t[:, tt:tt + 1], 1.0 / D), "ve")
                    v.drain()
                    p.tick(v.tensor_mul(st_c[:, tt:tt + 1], st_m[:, tt:tt + 1],
                                        st_m[:, tt:tt + 1]), "ve")
                    v.drain()
                    p.tick(v.tensor_sub(st_t[:, tt:tt + 1], st_t[:, tt:tt + 1],
                                        st_c[:, tt:tt + 1]), "ve")
                vd = p.cnt["ve"]

                @SEC("scalar")
                def _(s, tt=tt, vd=vd):
                    p.wait(s, "ve", vd)
                    if eps2 is not None:
                        bcol = 0 if eps2 == 1e-5 else 1
                        p.tick(s.activation(st_c[:, tt:tt + 1], st_t[:, tt:tt + 1],
                                            AF.Ln, scale=1.0 + eps2,
                                            bias=c_eps[:, bcol:bcol + 1]), "se")
                    else:
                        p.tick(s.activation(st_c[:, tt:tt + 1], st_t[:, tt:tt + 1],
                                            AF.Ln, bias=c_eps[:, 2:3]), "se")
                    s.drain()
                    p.tick(s.activation(st_c[:, tt:tt + 1], st_c[:, tt:tt + 1],
                                        AF.Exp, scale=-0.5), "se")
                    if keep_stats:
                        p.tick(s.activation(st_a[:, tt:tt + 1], st_t[:, tt:tt + 1],
                                            AF.Ln, bias=c_eps[:, 2:3]), "se")
                        s.drain()
                        p.tick(s.activation(st_a[:, tt:tt + 1], st_a[:, tt:tt + 1],
                                            AF.Exp, scale=-0.5), "se")
                sd = p.cnt["se"]

                @SEC("vector")
                def _(v, tt=tt, sd=sd):
                    p.wait(v, "se", sd)
                    if keep_stats:
                        p.tick(v.tensor_mul(st_b[:, tt:tt + 1], st_m[:, tt:tt + 1],
                                            st_a[:, tt:tt + 1]), "ve")
                        v.drain()
                        p.tick(v.tensor_scalar_add(st_a[:, tt:tt + 1],
                                                   st_a[:, tt:tt + 1], 1.0), "ve")
                    p.tick(v.tensor_scalar(scr, xa, st_m[:, tt:tt + 1],
                                           st_c[:, tt:tt + 1], ALU.subtract,
                                           ALU.mult), "ve")
                    if w_bcast is not None:
                        v.drain()
                        p.tick(v.tensor_mul(scr, scr, w_bcast), "ve")
                vd2 = p.cnt["ve"]

                @SEC("sync")
                def _(sy, tt=tt, vd2=vd2):
                    p.wait(sy, "ve", vd2)
                    p.dma(sy, "qs", dst_hbm[tt * 128:(tt + 1) * 128, :], scr)

        def emit_ag(ins_t, outs_t):
            @SEC("gpsimd")
            def _(g):
                p.wait(g, "qs")
                p.wait(g, "qg")
                g.collective_compute(
                    "AllGather", ALU.bypass, replica_groups=[list(range(NCORE))],
                    ins=[ins_t.ap().opt()], outs=[outs_t.ap().opt()],
                ).then_inc(p.sem["cc"], 1)
                p.cnt["cc"] += 1

        def join_all():
            for nm in ("tensor", "scalar", "gpsimd", "sync", "vector"):
                @SEC(nm)
                def _(e):
                    for k in ("te", "ve", "se", "qs", "qg", "cc"):
                        p.wait(e, k)

        with (
            nc.sbuf_tensor("c_rmsw1", [128, D], BF16) as c_rmsw1,
            nc.sbuf_tensor("p1x", [128, D], F32) as p1x,
            nc.sbuf_tensor("p1s", [128, D], BF16) as p1s,
        ):
            @SEC("gpsimd")
            def _(g):
                p.dma(g, "qg", c_rmsw1[:, :], rmsw1)
            emit_norm(x0, c_rmsw1[:, :], 1e-5, u_sl.ap(), p1x[:, :], p1s[:, :], True)
            emit_ag(u_sl, u_full)
            join_all()

        with (
            nc.sbuf_tensor("r_dt", [8, T], F32) as r_dt,
            nc.sbuf_tensor("r_el", [8, T], F32) as r_el,
        ):
            with (
                nc.sbuf_tensor("m_w", [128, 16 * 1160], BF16) as m_w,
                nc.sbuf_tensor("m_ut", [128, 2 * 16 * 512], BF16) as m_ut,
                nc.sbuf_tensor("m_st", [128, 512], BF16) as m_st,
                nc.sbuf_tensor("m_xs", [128, 512], BF16) as m_xs,
                nc.sbuf_tensor("r_wk", [8, T], F32) as r_wk,
                nc.psum_tensor("pz", [128, 512], F32) as pz,
                nc.psum_tensor("px", [128, 512], F32) as px,
                nc.psum_tensor("px2", [128, 512], F32) as px2,
            ):
                @SEC("gpsimd")
                def _(g):
                    for k in range(16):
                        p.dma(g, "qg", m_w[:, k * 1160:(k + 1) * 1160],
                              wmam[k * 128:(k + 1) * 128, :])
                wld = p.cnt["qg"]

                for nchk in range(16):
                    par = nchk % 2
                    t0 = nchk * 512

                    @SEC("sync")
                    def _(sy, par=par, t0=t0):
                        p.wait(sy, "te")
                        for k in range(16):
                            p.dma(sy, "qs",
                                  m_ut[:, (par * 16 + k) * 512:(par * 16 + k + 1) * 512],
                                  u_full.ap()[t0:t0 + 512, k * 128:(k + 1) * 128],
                                  transpose=True)
                    utld = p.cnt["qs"]

                    def UTK(k, par=par):
                        return m_ut[:, (par * 16 + k) * 512:(par * 16 + k + 1) * 512]

                    for ts_ in range(4):
                        @SEC("tensor")
                        def _(t, ts_=ts_, utld=utld, nchk=nchk, UTK=UTK):
                            p.wait(t, "qs", utld)
                            if nchk == 0:
                                p.wait(t, "qg", wld)
                            p.wait(t, "se")
                            for k in range(16):
                                mm = t.matmul(pz[:, :],
                                              UTK(k)[:, ts_ * 128:(ts_ + 1) * 128],
                                              m_w[:, k * 1160:k * 1160 + 512],
                                              start=(k == 0), stop=(k == 15))
                            p.tick(mm, "te")
                        tev = p.cnt["te"]

                        @SEC("scalar")
                        def _(s, tev=tev):
                            p.wait(s, "te", tev)
                            p.wait(s, "qs")
                            p.tick(s.activation(m_st[:, :], pz[:, :], AF.Silu), "se")
                        sev = p.cnt["se"]

                        @SEC("sync")
                        def _(sy, t0=t0, ts_=ts_, sev=sev):
                            p.wait(sy, "se", sev)
                            p.dma(sy, "qs",
                                  siluz.ap()[t0 + ts_ * 128:t0 + (ts_ + 1) * 128, :],
                                  m_st[:, :])

                    for m in range(7):
                        mw0 = [512, 640, 768, 896, 1024, 1088, 1152][m]
                        mcols = [128, 128, 128, 128, 64, 64, 8][m]

                        @SEC("tensor")
                        def _(t, m=m, mw0=mw0, mcols=mcols, UTK=UTK):
                            p.wait(t, "ve")
                            pb = px if (m % 2 == 0) else px2
                            for k in range(16):
                                mm = t.matmul(pb[0:mcols, :],
                                              m_w[:, k * 1160 + mw0:k * 1160 + mw0 + mcols],
                                              UTK(k), start=(k == 0), stop=(k == 15))
                            p.tick(mm, "te")
                        tev = p.cnt["te"]

                        if m < 6:
                            @SEC("vector")
                            def _(v, m=m, tev=tev):
                                p.wait(v, "te", tev)
                                p.wait(v, "qs")
                                pb = px if (m % 2 == 0) else px2
                                p.tick(v.tensor_copy(m_xs[:, :], pb[:, :]), "ve")
                            vev = p.cnt["ve"]

                            @SEC("sync")
                            def _(sy, m=m, t0=t0, vev=vev):
                                p.wait(sy, "ve", vev)
                                p.dma(sy, "qs", xbcd.ap()[m, :, t0:t0 + 512],
                                      m_xs[:, :])
                        else:
                            @SEC("vector")
                            def _(v, t0=t0, tev=tev):
                                p.wait(v, "te", tev)
                                p.tick(v.tensor_copy(r_dt[:, t0:t0 + 512],
                                                     px[0:8, :]), "ve")

                @SEC("vector")
                def _(v):
                    p.tick(v.tensor_scalar(r_wk[:, :], r_dt[:, :],
                                           c_small[0:8, 0:1], 0.0,
                                           ALU.add, ALU.max), "ve")
                relu_d = p.cnt["ve"]

                @SEC("scalar")
                def _(s, relu_d=relu_d):
                    p.wait(s, "ve", relu_d)
                    p.tick(s.activation(r_el[:, :], r_dt[:, :], AF.Abs,
                                        bias=c_small[0:8, 0:1]), "se")
                    s.drain()
                    p.tick(s.activation(r_el[:, :], r_el[:, :], AF.Exp,
                                        scale=-1.0), "se")
                    s.drain()
                    p.tick(s.activation(r_el[:, :], r_el[:, :], AF.Ln,
                                        bias=c_eps[0:8, 3:4]), "se")
                sp_d = p.cnt["se"]

                @SEC("vector")
                def _(v, sp_d=sp_d):
                    p.wait(v, "se", sp_d)
                    p.tick(v.tensor_add(r_dt[:, :], r_wk[:, :], r_el[:, :]), "ve")
                    v.drain()
                    p.tick(v.tensor_scalar_mul(r_wk[:, :], r_dt[:, :],
                                               c_small[0:8, 2:3]), "ve")
                    v.drain()
                    for b_ in range(B):
                        for c_ in range(NCH):
                            t0 = b_ * L + c_ * Q
                            p.tick(v.tensor_tensor_scan(
                                r_el[:, t0:t0 + Q], r_wk[:, t0:t0 + Q],
                                r_wk[:, t0:t0 + Q], 0.0, ALU.add, ALU.bypass), "ve")
                ell_d = p.cnt["ve"]

                @SEC("sync")
                def _(sy, ell_d=ell_d):
                    p.wait(sy, "ve", ell_d)
                    p.dma(sy, "qs", ellh.ap(), r_el[:, :])
                join_all()

            with (
                nc.sbuf_tensor("cvx", [128, T], BF16) as cvx,
                nc.sbuf_tensor("cva", [128, T], F32) as cva,
                nc.sbuf_tensor("cvo", [128, T], BF16) as cvo,
            ):
                for m in range(6):
                    @SEC("gpsimd")
                    def _(g, m=m):
                        p.wait(g, "ve")
                        p.dma(g, "qg", cvx[:, :], xbcd.ap()[m, :, :])
                    ld = p.cnt["qg"]

                    @SEC("vector")
                    def _(v, m=m, ld=ld):
                        p.wait(v, "qg", ld)
                        p.wait(v, "se")
                        for b_ in range(B):
                            s0, s1 = b_ * L, (b_ + 1) * L
                            p.tick(v.tensor_scalar_mul(
                                cva[:, s0:s1], cvx[:, s0:s1],
                                c_convw[:, m * 4 + 3:m * 4 + 4]), "ve")
                            for j in range(3):
                                sh = 3 - j
                                v.drain()
                                p.tick(v.scalar_tensor_tensor(
                                    cva[:, s0 + sh:s1], cvx[:, s0:s1 - sh],
                                    c_convw[:, m * 4 + j:m * 4 + j + 1],
                                    cva[:, s0 + sh:s1], ALU.mult, ALU.add), "ve")
                    cvd = p.cnt["ve"]

                    @SEC("scalar")
                    def _(s, m=m, cvd=cvd):
                        p.wait(s, "ve", cvd)
                        p.wait(s, "qs")
                        p.tick(s.activation(cvo[:, :], cva[:, :], AF.Silu,
                                            bias=c_convb[:, m:m + 1]), "se")
                    sv = p.cnt["se"]

                    @SEC("sync")
                    def _(sy, m=m, sv=sv):
                        p.wait(sy, "se", sv)
                        p.dma(sy, "qs", xbcc.ap()[m, :, :], cvo[:, :])
                join_all()

            SSTK = contextlib.ExitStack()
            with SSTK:
                EC2 = SSTK.enter_context
                sxch = EC2(nc.sbuf_tensor("sxch", [128, 2 * 768], BF16))
                ssz = EC2(nc.sbuf_tensor("ssz", [128, 2 * 512], BF16))
                selb = EC2(nc.sbuf_tensor("selb", [128, 2 * 1024], F32))
                sexpo = EC2(nc.sbuf_tensor("sexpo", [128, 1024], F32))
                seall = EC2(nc.sbuf_tensor("seall", [128, 1024], BF16))
                seb = EC2(nc.sbuf_tensor("seb", [64, 1024], BF16))
                smt = EC2(nc.sbuf_tensor("smt", [128, 128], BF16))
                swt = EC2(nc.sbuf_tensor("swt", [128, 1024], BF16))
                swts = EC2(nc.sbuf_tensor("swts", [64, 1024], BF16))
                sxtok = EC2(nc.sbuf_tensor("sxtok", [128, 512], BF16))
                sbct = EC2(nc.sbuf_tensor("sbct", [128, 64], BF16))
                sdet = EC2(nc.sbuf_tensor("sdet", [128, 16], F32))
                sds = EC2(nc.sbuf_tensor("sds", [128, 8], F32))
                seq = EC2(nc.sbuf_tensor("seq", [64, 8], F32))
                sxs = EC2(nc.sbuf_tensor("sxs", [128, 512], BF16))
                sei = EC2(nc.sbuf_tensor("sei", [64, 512], BF16))
                ss = EC2(nc.sbuf_tensor("ss", [64, 512], BF16))
                syg = EC2(nc.sbuf_tensor("syg", [128, 512], BF16))
                spa = EC2(nc.psum_tensor("spa", [128, 512], F32))
                spy = EC2(nc.psum_tensor("spy", [128, 512], F32))
                sps = EC2(nc.psum_tensor("sps", [64, 512], F32))
                spt = EC2(nc.psum_tensor("spt", [128, 1024], BF16))
                for b_ in range(B):
                    @SEC("vector")
                    def _(v):
                        p.tick(v.memset(ss[:, :], 0.0), "ve")
                    for c_ in range(NCH):
                        it = b_ * NCH + c_
                        t0 = b_ * L + c_ * Q
                        par = it % 2
                        XCH = sxch[:, par * 768:par * 768 + 768]
                        ELB = selb[:, par * 1024:(par + 1) * 1024]
                        SZ = ssz[:, par * 512:(par + 1) * 512]

                        @SEC("gpsimd")
                        def _(g, t0=t0, par=par, XCH=XCH, ELB=ELB):
                            p.wait(g, "ve")
                            p.dma(g, "qg", XCH,
                                  xbcc.ap().rearrange("v p t -> p v t")[:, :, t0:t0 + 128])
                            p.dma(g, "qg", ELB,
                                  bass.AP(ellh.ap().tensor, t0,
                                          [[0, 128], [T, 8], [1, 128]]))
                        ldx = p.cnt["qg"]

                        @SEC("sync")
                        def _(sy, t0=t0, SZ=SZ):
                            p.wait(sy, "ve")
                            p.dma(sy, "qs", SZ, siluz.ap()[t0:t0 + 128, :])
                        ldz = p.cnt["qs"]

                        @SEC("tensor")
                        def _(t, t0=t0, ldx=ldx, XCH=XCH):
                            p.wait(t, "qg", ldx)
                            p.wait(t, "ve")
                            t.transpose(spa[:, 0:8], r_el[:, t0:t0 + 128], c_idf[0:8, 0:8])
                            t.transpose(spa[:, 8:16], r_dt[:, t0:t0 + 128], c_idf[0:8, 0:8])
                            for m in range(4):
                                t.transpose(spt[:, m * 128:(m + 1) * 128],
                                            XCH[:, m * 128:(m + 1) * 128], c_idb[:, :])
                            t.transpose(spt[:, 512:640], XCH[:, 512:640], c_idb[:, :])
                            mm = t.matmul(spa[:, 128:256], XCH[0:64, 512:640],
                                          XCH[0:64, 640:768], start=True, stop=True)
                            p.tick(mm, "te")
                        te1 = p.cnt["te"]

                        @SEC("vector")
                        def _(v, te1=te1, ELB=ELB):
                            p.wait(v, "te", te1)
                            p.tick(v.tensor_copy(sdet[:, :], spa[:, 0:16]), "ve")
                            p.tick(v.tensor_copy(sxtok[:, :], spt[:, 0:512]), "ve")
                            p.tick(v.tensor_copy(sbct[:, :], spt[:, 512:576]), "ve")
                            p.tick(v.tensor_mul(smt[:, :], spa[:, 128:256],
                                                c_tri[:, :]), "ve")
                            v.drain()
                            for h in range(8):
                                p.tick(v.tensor_scalar(
                                    sexpo[:, h * 128:(h + 1) * 128],
                                    ELB[:, h * 128:(h + 1) * 128],
                                    sdet[:, h:h + 1], 0.0,
                                    ALU.subtract, ALU.min), "ve")
                        ve1 = p.cnt["ve"]

                        @SEC("scalar")
                        def _(s, ve1=ve1, ELB=ELB):
                            p.wait(s, "ve", ve1)
                            p.tick(s.activation(seall[:, :], sexpo[:, :], AF.Exp), "se")
                            p.tick(s.activation(seb[:, :], ELB[0:64, :], AF.Exp), "se")
                        se1 = p.cnt["se"]

                        @SEC("vector")
                        def _(v, se1=se1, XCH=XCH):
                            p.wait(v, "se", se1)
                            p.tick(v.tensor_copy(
                                seq[:, :],
                                seb[0:64, :].rearrange("p (h q) -> p h q", q=128)[:, :, 127:128]), "ve")
                            for h in range(8):
                                Eh = seall[:, h * 128:(h + 1) * 128]
                                p.tick(v.scalar_tensor_tensor(
                                    swt[:, h * 128:(h + 1) * 128], Eh,
                                    sdet[:, 8 + h:9 + h], smt[:, :],
                                    ALU.mult, ALU.mult), "ve")
                                p.tick(v.tensor_mul(
                                    sds[:, h:h + 1], sdet[:, 8 + h:9 + h],
                                    seall[:, h * 128 + 127:h * 128 + 128]), "ve")
                            v.drain()
                            for h in range(8):
                                p.tick(v.scalar_tensor_tensor(
                                    swt[:, h * 128:(h + 1) * 128], c_idb[:, :],
                                    c_dbc[:, h:h + 1], swt[:, h * 128:(h + 1) * 128],
                                    ALU.mult, ALU.add), "ve")
                                p.tick(v.tensor_mul(
                                    swts[0:64, h * 128:(h + 1) * 128],
                                    XCH[0:64, 640:768],
                                    seb[0:64, h * 128:(h + 1) * 128]), "ve")
                                p.tick(v.tensor_scalar_mul(
                                    sxs[:, h * 64:(h + 1) * 64],
                                    sxtok[:, h * 64:(h + 1) * 64],
                                    sds[:, h:h + 1]), "ve")
                                p.tick(v.tensor_scalar_mul(
                                    sei[0:64, h * 64:(h + 1) * 64],
                                    c_idb[0:64, 0:64],
                                    seq[:, h:h + 1]), "ve")
                        ve2 = p.cnt["ve"]

                        @SEC("tensor")
                        def _(t, ve2=ve2):
                            p.wait(t, "ve", ve2)
                            for h in range(8):
                                t.matmul(spy[:, h * 64:(h + 1) * 64],
                                         swt[:, h * 128:(h + 1) * 128],
                                         sxtok[:, h * 64:(h + 1) * 64],
                                         start=True, stop=False)
                                mm = t.matmul(spy[:, h * 64:(h + 1) * 64],
                                              swts[0:64, h * 128:(h + 1) * 128],
                                              ss[:, h * 64:(h + 1) * 64],
                                              start=False, stop=True)
                            p.tick(mm, "te")
                        tey = p.cnt["te"]

                        @SEC("tensor")
                        def _(t):
                            for h in range(8):
                                t.matmul(sps[:, h * 64:(h + 1) * 64],
                                         sbct[:, :],
                                         sxs[:, h * 64:(h + 1) * 64],
                                         start=True, stop=False)
                                mm = t.matmul(sps[:, h * 64:(h + 1) * 64],
                                              sei[0:64, h * 64:(h + 1) * 64],
                                              ss[:, h * 64:(h + 1) * 64],
                                              start=False, stop=True)
                            p.tick(mm, "te")
                        tes = p.cnt["te"]

                        @SEC("vector")
                        def _(v, tey=tey, tes=tes, ldz=ldz, SZ=SZ):
                            p.wait(v, "te", tey)
                            p.wait(v, "qs", ldz)
                            p.tick(v.tensor_mul(syg[:, :], spy[:, :], SZ), "ve")
                            p.wait(v, "te", tes)
                            p.tick(v.tensor_copy(ss[:, :], sps[:, :]), "ve")
                        vey = p.cnt["ve"]

                        @SEC("sync")
                        def _(sy, t0=t0, vey=vey):
                            p.wait(sy, "ve", vey)
                            p.dma(sy, "qs", yg_sl.ap()[t0:t0 + 128, :], syg[:, :])
                join_all()

        emit_ag(yg_sl, yg_full)

        with (
            nc.sbuf_tensor("c_nw", [128, DI], BF16) as c_nw,
            nc.sbuf_tensor("fyg", [128, 8 * DI], BF16) as fyg,
            nc.sbuf_tensor("fsc", [128, DI], BF16) as fsc,
        ):
            @SEC("gpsimd")
            def _(g):
                p.dma(g, "qg", c_nw[:, :], normw)
                p.wait(g, "cc")
                fygv = fyg[:, :].rearrange("p (tt c) -> p tt c", c=DI)
                qg_base = p.cnt["qg"]
                for arm in nc.switch_core_id(engines=[g], n=128):
                    r = arm.logical % NCORE
                    for src in range(NCORE):
                        g.wait_ge(p.sem["qg"], qg_base + src * 16)
                        g.dma_start(
                            out=fygv[:, :, src * ZC:(src + 1) * ZC],
                            in_=yg_full.ap()[src, r * TPC:(r + 1) * TPC, :]
                            .rearrange("(tt pp) z -> pp tt z", pp=128),
                        ).then_inc(p.sem["qg"], 16)
                p.cnt["qg"] += 8 * 16
            gath = p.cnt["qg"]

            for tt in range(8):
                @SEC("scalar")
                def _(s, tt=tt, gath=gath):
                    p.wait(s, "qg", gath)
                    p.wait(s, "qs")
                    p.tick(s.activation(fsc[:, :], fyg[:, tt * DI:(tt + 1) * DI],
                                        AF.Square, accum_out=st_t[:, tt:tt + 1]), "se")
                    s.drain()
                    p.tick(s.activation(st_c[:, tt:tt + 1], st_t[:, tt:tt + 1],
                                        AF.Ln, scale=1.0 / DI,
                                        bias=c_eps[:, 2:3]), "se")
                    s.drain()
                    p.tick(s.activation(st_c[:, tt:tt + 1], st_c[:, tt:tt + 1],
                                        AF.Exp, scale=-0.5), "se")
                sev = p.cnt["se"]

                @SEC("vector")
                def _(v, tt=tt, sev=sev):
                    p.wait(v, "se", sev)
                    p.tick(v.tensor_scalar_mul(fsc[:, :], fyg[:, tt * DI:(tt + 1) * DI],
                                               st_c[:, tt:tt + 1]), "ve")
                    v.drain()
                    p.tick(v.tensor_mul(fsc[:, :], fsc[:, :], c_nw[:, :]), "ve")
                vev = p.cnt["ve"]

                @SEC("sync")
                def _(sy, tt=tt, vev=vev):
                    p.wait(sy, "ve", vev)
                    p.dma(sy, "qs", yn_h.ap()[tt * 128:(tt + 1) * 128, :], fsc[:, :])
            ynst = p.cnt["qs"]
            join_all()

        with (
            nc.sbuf_tensor("fw", [128, 32 * 2048], BF16) as fw,
            nc.sbuf_tensor("fyt", [128, 32 * 128], BF16) as fyt,
            nc.sbuf_tensor("fx0", [128, D], F32) as fx0,
            nc.sbuf_tensor("fxr", [128, D], F32) as fxr,
            nc.sbuf_tensor("fx1", [128, D], F32) as fx1,
            nc.psum_tensor("fp", [128, 512], F32) as fp,
            nc.psum_tensor("fp2", [128, 512], F32) as fp2,
        ):
            @SEC("gpsimd")
            def _(g):
                for k in range(32):
                    p.dma(g, "qg", fw[:, k * 2048:(k + 1) * 2048],
                          wop[k * 128:(k + 1) * 128, :])
            wld = p.cnt["qg"]

            for tt in range(8):
                @SEC("sync")
                def _(sy, tt=tt):
                    p.wait(sy, "te")
                    for k in range(32):
                        p.dma(sy, "qs", fyt[:, k * 128:(k + 1) * 128],
                              yn_h.ap()[tt * 128:(tt + 1) * 128,
                                        k * 128:(k + 1) * 128],
                              transpose=True)
                ytld = p.cnt["qs"]

                @SEC("gpsimd")
                def _(g, tt=tt):
                    p.wait(g, "ve")
                    p.dma(g, "qg", fx0[:, :], x0[tt * 128:(tt + 1) * 128, :])
                x0ld = p.cnt["qg"]

                @SEC("vector")
                def _(v, tt=tt, x0ld=x0ld):
                    p.wait(v, "qg", x0ld)
                    p.tick(v.tensor_scalar(fxr[:, :], fx0[:, :], st_a[:, tt:tt + 1],
                                           st_b[:, tt:tt + 1], ALU.mult,
                                           ALU.subtract), "ve")
                xrd = p.cnt["ve"]

                for dm in range(4):
                    pb = fp if dm % 2 == 0 else fp2

                    @SEC("tensor")
                    def _(t, tt=tt, dm=dm, pb=pb, ytld=ytld):
                        p.wait(t, "qs", ytld)
                        if tt == 0 and dm == 0:
                            p.wait(t, "qg", wld)
                        p.wait(t, "ve")
                        for k in range(32):
                            mm = t.matmul(pb[:, :], fyt[:, k * 128:(k + 1) * 128],
                                          fw[:, k * 2048 + dm * 512:k * 2048 + dm * 512 + 512],
                                          start=(k == 0), stop=(k == 31))
                        p.tick(mm, "te")
                    tev = p.cnt["te"]

                    @SEC("vector")
                    def _(v, tt=tt, dm=dm, pb=pb, tev=tev, xrd=xrd):
                        p.wait(v, "te", tev)
                        p.wait(v, "ve", xrd)
                        if dm == 0:
                            p.wait(v, "qs")
                        p.tick(v.scalar_tensor_tensor(
                            fx1[:, dm * 512:(dm + 1) * 512], pb[:, :], 1.0,
                            fxr[:, dm * 512:(dm + 1) * 512], ALU.mult, ALU.add), "ve")
                vd = p.cnt["ve"]

                @SEC("sync")
                def _(sy, tt=tt, vd=vd):
                    p.wait(sy, "ve", vd)
                    p.dma(sy, "qs", x1h.ap()[tt * 128:(tt + 1) * 128, :], fx1[:, :])
            join_all()

        def emit_ffn(x_in, x_out, idx):
            with (
                nc.sbuf_tensor("c_fw%d" % idx, [128, D], BF16) as c_fw,
                nc.sbuf_tensor("q_x%d" % idx, [128, D], F32) as q_x,
                nc.sbuf_tensor("q_s%d" % idx, [128, D], BF16) as q_s,
            ):
                @SEC("gpsimd")
                def _(g):
                    p.dma(g, "qg", c_fw[:, :], ffnw)
                emit_norm(x_in, c_fw[:, :], 1e-6, hh.ap(), q_x[:, :], q_s[:, :], False)
                join_all()

            FSTK = contextlib.ExitStack()
            with FSTK:
                EC3 = FSTK.enter_context
                f_ht = EC3(nc.sbuf_tensor("f_ht%d" % idx, [128, 16 * 1024], BF16))
                f_rw = EC3(nc.sbuf_tensor("f_rw%d" % idx, [128, 16 * 8], BF16))
                f_gu = EC3(nc.sbuf_tensor("f_gu%d" % idx, [128, 2 * 2 * 16 * 256], BF16))
                f_wd = EC3(nc.sbuf_tensor("f_wd%d" % idx, [128, 8 * 2048], BF16))
                f_act = EC3(nc.sbuf_tensor("f_act%d" % idx, [128, 8 * 1024], BF16))
                f_sil = EC3(nc.sbuf_tensor("f_sil%d" % idx, [128, 512], BF16))
                f_moe = EC3(nc.sbuf_tensor("f_moe%d" % idx, [128, 8 * 2048], F32))
                f_p8 = EC3(nc.sbuf_tensor("f_p8%d" % idx, [128, 8], F32))
                f_q8 = EC3(nc.sbuf_tensor("f_q8%d" % idx, [128, 8], F32))
                f_xi = EC3(nc.sbuf_tensor("f_xi%d" % idx, [128, D], F32))
                f_pg = EC3(nc.psum_tensor("f_pg%d" % idx, [128, 512], F32))
                f_pu = EC3(nc.psum_tensor("f_pu%d" % idx, [128, 512], F32))
                f_pd = EC3(nc.psum_tensor("f_pd%d" % idx, [128, 512], F32))
                f_pr = EC3(nc.psum_tensor("f_pr%d" % idx, [128, 512], F32))
                @SEC("sync")
                def _(sy):
                    for k in range(16):
                        p.dma(sy, "qs", f_ht[:, k * 1024:(k + 1) * 1024],
                              hh.ap()[:, k * 128:(k + 1) * 128], transpose=True)
                htld = p.cnt["qs"]

                @SEC("gpsimd")
                def _(g):
                    for k in range(16):
                        p.dma(g, "qg", f_rw[:, k * 8:(k + 1) * 8],
                              routw[k * 128:(k + 1) * 128, :])
                rwld = p.cnt["qg"]

                for tt in range(8):
                    @SEC("tensor")
                    def _(t, tt=tt, htld=htld, rwld=rwld):
                        p.wait(t, "qs", htld)
                        p.wait(t, "qg", rwld)
                        p.wait(t, "ve")
                        for k in range(16):
                            mm = t.matmul(f_pr[:, 0:8],
                                          f_ht[:, k * 1024 + tt * 128:k * 1024 + tt * 128 + 128],
                                          f_rw[:, k * 8:(k + 1) * 8],
                                          start=(k == 0), stop=(k == 15))
                        p.tick(mm, "te")
                    tev = p.cnt["te"]

                    @SEC("vector")
                    def _(v, tt=tt, tev=tev):
                        p.wait(v, "te", tev)
                        p.tick(v.reduce_max(tp1a[:, :], f_pr[:, 0:8], axis=AX.X), "ve")
                        v.drain()
                        p.tick(v.tensor_scalar_mul(tp1a[:, :], tp1a[:, :], -1.0), "ve")
                    mxd = p.cnt["ve"]

                    @SEC("scalar")
                    def _(s, tt=tt, mxd=mxd):
                        p.wait(s, "ve", mxd)
                        p.tick(s.activation(f_p8[:, :], f_pr[:, 0:8], AF.Exp,
                                            bias=tp1a[:, :]), "se")
                    exd = p.cnt["se"]

                    @SEC("vector")
                    def _(v, tt=tt, exd=exd):
                        p.wait(v, "se", exd)
                        p.tick(v.reduce_sum(tp1a[:, :], f_p8[:, :], axis=AX.X), "ve")
                        v.drain()
                        p.tick(v.reciprocal(tp1b[:, :], tp1a[:, :]), "ve")
                        v.drain()
                        p.tick(v.tensor_scalar_mul(f_p8[:, :], f_p8[:, :],
                                                   tp1b[:, :]), "ve")
                        v.drain()
                        p.tick(v.reduce_max(tp1a[:, :], f_p8[:, :], axis=AX.X), "ve")
                        v.drain()
                        p.tick(v.tensor_single_scalar(f_q8[:, :], f_p8[:, :],
                                                      tp1a[:, :], ALU.is_lt), "ve")
                        v.drain()
                        p.tick(v.tensor_mul(f_q8[:, :], f_q8[:, :], f_p8[:, :]), "ve")
                        v.drain()
                        p.tick(v.reduce_max(tp1b[:, :], f_q8[:, :], axis=AX.X), "ve")
                        v.drain()
                        p.tick(v.tensor_add(tp1a[:, :], tp1a[:, :], tp1b[:, :]), "ve")
                        v.drain()
                        p.tick(v.reciprocal(tp1a[:, :], tp1a[:, :]), "ve")
                        p.tick(v.tensor_single_scalar(f_q8[:, :], f_p8[:, :],
                                                      tp1b[:, :], ALU.is_ge), "ve")
                        v.drain()
                        p.tick(v.scalar_tensor_tensor(wtok[:, tt * 8:(tt + 1) * 8],
                                                      f_p8[:, :], tp1a[:, :],
                                                      f_q8[:, :], ALU.mult,
                                                      ALU.mult), "ve")
                rt_done = p.cnt["ve"]

                for es in range(-1, E):
                    for fch in range(4):
                        gu0 = (fch % 2) * 8192

                        @SEC("gpsimd")
                        def _(g, es=es, fch=fch, gu0=gu0):
                            p.wait(g, "te")
                            c0, c1 = fch * 256, (fch + 1) * 256
                            for k in range(16):
                                if es < 0:
                                    gs = swg[k * 128:(k + 1) * 128, c0:c1]
                                    us = swu[k * 128:(k + 1) * 128, c0:c1]
                                else:
                                    gs = wgate[es, k * 128:(k + 1) * 128, c0:c1]
                                    us = wup[es, k * 128:(k + 1) * 128, c0:c1]
                                p.dma(g, "qg",
                                      f_gu[:, gu0 + k * 256:gu0 + (k + 1) * 256], gs)
                                p.dma(g, "qg",
                                      f_gu[:, gu0 + 4096 + k * 256:gu0 + 4096 + (k + 1) * 256], us)
                        guld = p.cnt["qg"]

                        for mf in range(2):
                            fglob = fch * 2 + mf
                            for tch in range(2):
                                @SEC("tensor")
                                def _(t, gu0=gu0, mf=mf, tch=tch, guld=guld,
                                      htld=htld):
                                    p.wait(t, "qg", guld)
                                    p.wait(t, "qs", htld)
                                    p.wait(t, "ve")
                                    off = gu0 + mf * 128
                                    for k in range(16):
                                        t.matmul(f_pg[:, :],
                                                 f_gu[:, off + k * 256:off + k * 256 + 128],
                                                 f_ht[:, k * 1024 + tch * 512:k * 1024 + (tch + 1) * 512],
                                                 start=(k == 0), stop=(k == 15))
                                    offu = gu0 + 4096 + mf * 128
                                    for k in range(16):
                                        mm = t.matmul(f_pu[:, :],
                                                      f_gu[:, offu + k * 256:offu + k * 256 + 128],
                                                      f_ht[:, k * 1024 + tch * 512:k * 1024 + (tch + 1) * 512],
                                                      start=(k == 0), stop=(k == 15))
                                    p.tick(mm, "te")
                                tev = p.cnt["te"]

                                @SEC("scalar")
                                def _(s, tev=tev):
                                    p.wait(s, "te", tev)
                                    p.wait(s, "ve")
                                    p.tick(s.activation(f_sil[:, :], f_pg[:, :],
                                                        AF.Silu), "se")
                                sev = p.cnt["se"]

                                @SEC("vector")
                                def _(v, fglob=fglob, tch=tch, sev=sev):
                                    p.wait(v, "se", sev)
                                    p.tick(v.tensor_mul(
                                        f_act[:, fglob * 1024 + tch * 512:
                                              fglob * 1024 + (tch + 1) * 512],
                                        f_sil[:, :], f_pu[:, :]), "ve")
                    act_done = p.cnt["ve"]

                    @SEC("gpsimd")
                    def _(g, es=es):
                        p.wait(g, "te")
                        for k in range(8):
                            if es < 0:
                                p.dma(g, "qg", f_wd[:, k * 2048:(k + 1) * 2048],
                                      swd[k * 128:(k + 1) * 128, :])
                            else:
                                p.dma(g, "qg", f_wd[:, k * 2048:(k + 1) * 2048],
                                      wdown[es, k * 128:(k + 1) * 128, :])
                    wdld = p.cnt["qg"]

                    for tch in range(2):
                        for tsub in range(4):
                            gt = tch * 4 + tsub
                            for dm in range(4):
                                @SEC("tensor")
                                def _(t, es=es, gt=gt, tch=tch, tsub=tsub, dm=dm,
                                      wdld=wdld, act_done=act_done):
                                    p.wait(t, "qg", wdld)
                                    p.wait(t, "ve", act_done)
                                    p.wait(t, "ve")
                                    for k in range(8):
                                        mm = t.matmul(
                                            f_pd[:, :],
                                            f_act[:, k * 1024 + tch * 512 + tsub * 128:
                                                  k * 1024 + tch * 512 + (tsub + 1) * 128],
                                            f_wd[:, k * 2048 + dm * 512:
                                                 k * 2048 + (dm + 1) * 512],
                                            start=(k == 0), stop=(k == 7))
                                    p.tick(mm, "te")
                                tev = p.cnt["te"]

                                @SEC("vector")
                                def _(v, es=es, gt=gt, dm=dm, tev=tev,
                                      rt_done=rt_done):
                                    p.wait(v, "te", tev)
                                    dst = f_moe[:, gt * 2048 + dm * 512:
                                                gt * 2048 + (dm + 1) * 512]
                                    if es < 0:
                                        p.tick(v.tensor_copy(dst, f_pd[:, :]), "ve")
                                    else:
                                        p.wait(v, "ve", rt_done)
                                        p.tick(v.scalar_tensor_tensor(
                                            dst, f_pd[:, :],
                                            wtok[:, gt * 8 + es:gt * 8 + es + 1],
                                            dst, ALU.mult, ALU.add), "ve")

                for tt in range(8):
                    @SEC("gpsimd")
                    def _(g, tt=tt):
                        p.wait(g, "ve")
                        p.wait(g, "qs")
                        p.dma(g, "qg", f_xi[:, :], x_in[tt * 128:(tt + 1) * 128, :])
                    xild = p.cnt["qg"]

                    @SEC("vector")
                    def _(v, tt=tt, xild=xild):
                        p.wait(v, "qg", xild)
                        p.tick(v.tensor_add(f_xi[:, :], f_xi[:, :],
                                            f_moe[:, tt * 2048:(tt + 1) * 2048]), "ve")
                    vad = p.cnt["ve"]

                    @SEC("sync")
                    def _(sy, tt=tt, vad=vad):
                        p.wait(sy, "ve", vad)
                        p.dma(sy, "qs", x_out[tt * 128:(tt + 1) * 128, :], f_xi[:, :])
                join_all()

        emit_ffn(x1h.ap(), x2h.ap(), 0)

        with (
            nc.sbuf_tensor("a_x", [128, D], F32) as a_x,
            nc.sbuf_tensor("a_s", [128, D], BF16) as a_s,
        ):
            emit_norm(x2h.ap(), None, None, u2_sl.ap(), a_x[:, :], a_s[:, :], False)
            emit_ag(u2_sl, u2_full)
            join_all()

        ASTK = contextlib.ExitStack()
        with ASTK:
            EC4 = ASTK.enter_context
            a_w = EC4(nc.sbuf_tensor("a_w", [128, 3 * 16 * 256], BF16))
            a_dq1 = EC4(nc.sbuf_tensor("a_dq1", [128, 256], F32))
            a_dq2 = EC4(nc.sbuf_tensor("a_dq2", [128, 256], F32))
            a_bqk = EC4(nc.sbuf_tensor("a_bqk", [128, 4], F32))
            a_bvb = EC4(nc.sbuf_tensor("a_bvb", [128, 256], F32))
            a_q = EC4(nc.sbuf_tensor("a_q", [128, 2 * T], BF16))
            a_k = EC4(nc.sbuf_tensor("a_k", [128, 2 * T], BF16))
            a_v = EC4(nc.sbuf_tensor("a_v", [128, 2 * T], BF16))
            a_kc = EC4(nc.sbuf_tensor("a_kc", [128, L], BF16))
            a_ut = EC4(nc.sbuf_tensor("a_ut", [128, 2 * 16 * 512], BF16))
            a_pt = EC4(nc.sbuf_tensor("a_pt", [128, 2 * 512], BF16))
            a_rd = EC4(nc.sbuf_tensor("a_rd", [1, 512], F32))
            a_rdb = EC4(nc.sbuf_tensor("a_rdb", [128, 512], F32))
            a_ao = EC4(nc.sbuf_tensor("a_ao", [128, 512], BF16))
            a_ps1 = EC4(nc.psum_tensor("a_ps1", [128, 512], F32))
            a_ps2 = EC4(nc.psum_tensor("a_ps2", [128, 512], F32))
            a_pv = EC4(nc.psum_tensor("a_pv", [128, 512], F32))
            a_pd = EC4(nc.psum_tensor("a_pd", [128, 512], F32))
            a_pr = EC4(nc.psum_tensor("a_pr", [128, 512], F32))
            @SEC("gpsimd")
            def _(g):
                p.dma(g, "qg", a_bqk[:, :], bqk)
                p.dma(g, "qg", a_bvb[:, :], bvb)
            for mtx in range(3):
                for k in range(16):
                    @SEC("gpsimd")
                    def _(g, mtx=mtx, k=k):
                        p.wait(g, "ve")
                        p.dma(g, "qg", a_dq1[:, :], wqkv[mtx, k * 128:(k + 1) * 128, :])
                        p.dma(g, "qg", a_dq2[:, :], wqkvs[mtx, k * 128:(k + 1) * 128, :])
                    dqld = p.cnt["qg"]

                    @SEC("vector")
                    def _(v, mtx=mtx, k=k, dqld=dqld):
                        p.wait(v, "qg", dqld)
                        p.tick(v.tensor_mul(
                            a_w[:, (mtx * 16 + k) * 256:(mtx * 16 + k + 1) * 256],
                            a_dq1[:, :], a_dq2[:, :]), "ve")
            wqd = p.cnt["ve"]

            @SEC("vector")
            def _(v, wqd=wqd):
                p.tick(v.tensor_scalar_mul(a_bqk[:, 0:2], a_bqk[:, 0:2], S128I), "ve")
            bqd = p.cnt["ve"]

            for nchk in range(16):
                par = nchk % 2
                t0 = nchk * 512

                @SEC("sync")
                def _(sy, par=par, t0=t0):
                    p.wait(sy, "te")
                    for k in range(16):
                        p.dma(sy, "qs",
                              a_ut[:, (par * 16 + k) * 512:(par * 16 + k + 1) * 512],
                              u2_full.ap()[t0:t0 + 512, k * 128:(k + 1) * 128],
                              transpose=True)
                utld = p.cnt["qs"]

                def UT2(k, par=par):
                    return a_ut[:, (par * 16 + k) * 512:(par * 16 + k + 1) * 512]

                for m in range(2):
                    @SEC("tensor")
                    def _(t, m=m, utld=utld, wqd=wqd, bqd=bqd, UT2=UT2):
                        p.wait(t, "qs", utld)
                        p.wait(t, "ve", bqd)
                        p.wait(t, "se")
                        for k in range(16):
                            mm = t.matmul(a_ps1[:, :],
                                          a_w[:, k * 256 + m * 128:k * 256 + m * 128 + 128],
                                          UT2(k), start=(k == 0), stop=(k == 15))
                        p.tick(mm, "te")
                        for k in range(16):
                            mm = t.matmul(a_ps2[:, :],
                                          a_w[:, (16 + k) * 256 + m * 128:(16 + k) * 256 + m * 128 + 128],
                                          UT2(k), start=(k == 0), stop=(k == 15))
                        p.tick(mm, "te")
                    teq = p.cnt["te"] - 1
                    tek = p.cnt["te"]

                    @SEC("scalar")
                    def _(s, m=m, t0=t0, teq=teq, tek=tek):
                        p.wait(s, "te", teq)
                        p.tick(s.activation(a_q[:, m * T + t0:m * T + t0 + 512],
                                            a_ps1[:, :], AF.Identity, scale=S128I,
                                            bias=a_bqk[:, m:m + 1]), "se")
                        p.wait(s, "te", tek)
                        p.tick(s.activation(a_k[:, m * T + t0:m * T + t0 + 512],
                                            a_ps2[:, :], AF.Identity,
                                            bias=a_bqk[:, 2 + m:3 + m]), "se")
                qk_se = p.cnt["se"]

                for tsub in range(4):
                    @SEC("tensor")
                    def _(t, tsub=tsub, utld=utld, UT2=UT2, qk_se=qk_se):
                        p.wait(t, "se", qk_se)
                        p.wait(t, "ve")
                        for k in range(16):
                            mm = t.matmul(a_pv[:, 0:256],
                                          UT2(k)[:, tsub * 128:(tsub + 1) * 128],
                                          a_w[:, (32 + k) * 256:(32 + k + 1) * 256],
                                          start=(k == 0), stop=(k == 15))
                        p.tick(mm, "te")
                    tev = p.cnt["te"]

                    @SEC("vector")
                    def _(v, t0=t0, tsub=tsub, tev=tev):
                        p.wait(v, "te", tev)
                        gt = (t0 + tsub * 128) // 128
                        p.tick(v.tensor_add(a_v[:, gt * 256:(gt + 1) * 256],
                                            a_pv[:, 0:256], a_bvb[:, :]), "ve")
            qkv_done = p.cnt["ve"]

            for b_ in range(B):
                for hl in range(2):
                    @SEC("vector")
                    def _(v, b_=b_, hl=hl, qkv_done=qkv_done):
                        p.wait(v, "ve", qkv_done)
                        p.wait(v, "te")
                        ks = a_k[:, hl * T + b_ * L:hl * T + (b_ + 1) * L]
                        p.tick(v.reduce_sum(tp1a[:, :], ks, axis=AX.X), "ve")
                        v.drain()
                        p.tick(v.tensor_scalar_mul(tp1a[:, :], tp1a[:, :],
                                                   1.0 / L), "ve")
                        v.drain()
                        p.tick(v.tensor_single_scalar(a_kc[:, :], ks, tp1a[:, :],
                                                      ALU.subtract), "ve")
                    kcd = p.cnt["ve"]

                    for qch in range(8):
                        q0 = b_ * L + qch * 512
                        se_base = p.cnt["se"]
                        score_te = []
                        pvd_te = []

                        @SEC("tensor")
                        def _(t, b_=b_, hl=hl, qch=qch, q0=q0, kcd=kcd,
                              se_base=se_base):
                            p.wait(t, "ve")
                            qsl = a_q[:, hl * T + q0:hl * T + q0 + 512]
                            for kt in range(33):
                                if kt < 32:
                                    psb = a_ps1 if kt % 2 == 0 else a_ps2
                                    if kt >= 2:
                                        t.wait_ge(p.sem["se"], se_base + kt - 1)
                                    mm = t.matmul(psb[:, :],
                                                  a_kc[:, kt * 128:(kt + 1) * 128],
                                                  qsl, start=True, stop=True,
                                                  skip_group_check=True)
                                    score_te.append(p.tick(mm, "te"))
                                if kt >= 1:
                                    kp = kt - 1
                                    t.wait_ge(p.sem["se"], se_base + kp + 1)
                                    ptb = a_pt[:, (kp % 2) * 512:(kp % 2 + 1) * 512]
                                    gt = b_ * 32 + kp
                                    t.matmul(a_pv[:, :],
                                             a_v[:, gt * 256 + hl * 128:gt * 256 + hl * 128 + 128],
                                             ptb, start=(kp == 0), stop=(kp == 31),
                                             skip_group_check=True)
                                    mm = t.matmul(a_pd[0:1, :], c_onesb[:, :], ptb,
                                                  start=(kp == 0), stop=(kp == 31),
                                                  skip_group_check=True)
                                    pvd_te.append(p.tick(mm, "te"))

                        @SEC("scalar")
                        def _(s, score_te=score_te, pvd_te=pvd_te):
                            for kt in range(32):
                                s.wait_ge(p.sem["te"], score_te[kt])
                                if kt >= 2:
                                    s.wait_ge(p.sem["te"], pvd_te[kt - 2])
                                psb = a_ps1 if kt % 2 == 0 else a_ps2
                                p.tick(s.activation(
                                    a_pt[:, (kt % 2) * 512:(kt % 2 + 1) * 512],
                                    psb[:, :], AF.Exp), "se")
                        att_te = p.cnt["te"]

                        @SEC("vector")
                        def _(v, att_te=att_te):
                            p.wait(v, "te", att_te)
                            p.tick(v.reciprocal(a_rd[:, :], a_pd[0:1, :]), "ve")
                        rdd = p.cnt["ve"]

                        @SEC("tensor")
                        def _(t, rdd=rdd):
                            p.wait(t, "ve", rdd)
                            mm = t.matmul(a_pr[:, :], c_onesf[:, :], a_rd[:, :],
                                          start=True, stop=True)
                            p.tick(mm, "te")
                        ter = p.cnt["te"]

                        @SEC("vector")
                        def _(v, ter=ter):
                            p.wait(v, "te", ter)
                            p.wait(v, "qs")
                            p.tick(v.tensor_copy(a_rdb[:, :], a_pr[:, :]), "ve")
                            v.drain()
                            p.tick(v.tensor_mul(a_ao[:, :], a_pv[:, :],
                                                a_rdb[:, :]), "ve")
                        aod = p.cnt["ve"]

                        @SEC("sync")
                        def _(sy, hl=hl, q0=q0, aod=aod):
                            p.wait(sy, "ve", aod)
                            p.dma(sy, "qs",
                                  ao_sl.ap()[hl * 128:(hl + 1) * 128, q0:q0 + 512],
                                  a_ao[:, :])
            join_all()

        emit_ag(ao_sl, ao_full)

        with (
            nc.sbuf_tensor("w_wo", [128, 16 * 2048], BF16) as w_wo,
            nc.sbuf_tensor("w_dq1", [128, 2048], F32) as w_dq1,
            nc.sbuf_tensor("w_dq2", [128, 2048], F32) as w_dq2,
            nc.sbuf_tensor("w_ac", [128, 16 * 1024], BF16) as w_ac,
            nc.sbuf_tensor("w_bo", [128, D], BF16) as w_bo,
            nc.sbuf_tensor("w_x2", [128, D], F32) as w_x2,
            nc.sbuf_tensor("w_x3", [128, D], F32) as w_x3,
            nc.psum_tensor("w_p", [128, 512], F32) as w_p,
            nc.psum_tensor("w_p2", [128, 512], F32) as w_p2,
        ):
            @SEC("gpsimd")
            def _(g):
                p.dma(g, "qg", w_bo[:, :], bob)
            for k in range(16):
                @SEC("gpsimd")
                def _(g, k=k):
                    p.wait(g, "ve")
                    p.dma(g, "qg", w_dq1[:, :], woc[k * 128:(k + 1) * 128, :])
                    p.dma(g, "qg", w_dq2[:, :], wos[k * 128:(k + 1) * 128, :])
                dqld = p.cnt["qg"]

                @SEC("vector")
                def _(v, k=k, dqld=dqld):
                    p.wait(v, "qg", dqld)
                    p.tick(v.tensor_mul(w_wo[:, k * 2048:(k + 1) * 2048],
                                        w_dq1[:, :], w_dq2[:, :]), "ve")
            wod = p.cnt["ve"]

            @SEC("gpsimd")
            def _(g):
                p.wait(g, "cc")
                wacv = w_ac[:, :].rearrange("p (kk c) -> p kk c", c=1024)
                qg_base = p.cnt["qg"]
                for arm in nc.switch_core_id(engines=[g], n=128):
                    r = arm.logical % NCORE
                    for src in range(NCORE):
                        g.wait_ge(p.sem["qg"], qg_base + src * 16)
                        g.dma_start(
                            out=wacv[:, 2 * src:2 * src + 2, :],
                            in_=ao_full.ap()[src, :, r * TPC:(r + 1) * TPC]
                            .rearrange("(kk pp) t -> pp kk t", pp=128),
                        ).then_inc(p.sem["qg"], 16)
                p.cnt["qg"] += 8 * 16
            acld = p.cnt["qg"]

            for tt in range(8):
                @SEC("gpsimd")
                def _(g, tt=tt):
                    p.wait(g, "ve")
                    p.dma(g, "qg", w_x2[:, :], x2h.ap()[tt * 128:(tt + 1) * 128, :])
                x2ld = p.cnt["qg"]

                for dm in range(4):
                    pb = w_p if dm % 2 == 0 else w_p2

                    @SEC("tensor")
                    def _(t, tt=tt, dm=dm, pb=pb, acld=acld, wod=wod):
                        p.wait(t, "qg", acld)
                        p.wait(t, "ve", wod)
                        p.wait(t, "ve")
                        for k in range(16):
                            mm = t.matmul(pb[:, :],
                                          w_ac[:, k * 1024 + tt * 128:k * 1024 + tt * 128 + 128],
                                          w_wo[:, k * 2048 + dm * 512:k * 2048 + (dm + 1) * 512],
                                          start=(k == 0), stop=(k == 15))
                        p.tick(mm, "te")
                    tev = p.cnt["te"]

                    @SEC("vector")
                    def _(v, tt=tt, dm=dm, pb=pb, tev=tev, x2ld=x2ld):
                        p.wait(v, "te", tev)
                        p.wait(v, "qg", x2ld)
                        if dm == 0:
                            p.wait(v, "qs")
                        p.tick(v.scalar_tensor_tensor(
                            w_x3[:, dm * 512:(dm + 1) * 512], pb[:, :], 1.0,
                            w_x2[:, dm * 512:(dm + 1) * 512], ALU.mult, ALU.add), "ve")
                        v.drain()
                        p.tick(v.tensor_add(w_x3[:, dm * 512:(dm + 1) * 512],
                                            w_x3[:, dm * 512:(dm + 1) * 512],
                                            w_bo[:, dm * 512:(dm + 1) * 512]), "ve")
                vd = p.cnt["ve"]

                @SEC("sync")
                def _(sy, tt=tt, vd=vd):
                    p.wait(sy, "ve", vd)
                    p.dma(sy, "qs", x3h.ap()[tt * 128:(tt + 1) * 128, :], w_x3[:, :])
            join_all()

        emit_ffn(x3h.ap(), out, 1)

        if dbg_outs:
            @SEC("gpsimd")
            def _(g):
                p.wait(g, "qs")
                for name, (dst, src) in dbg_outs.items():
                    p.dma(g, "qg", dst, src.ap())

        @SEC("sync")
        def _(sy):
            p.wait(sy, "qs")
            p.wait(sy, "qg")

    return nc


_NC_CACHE = None


def prep_inputs(L0_attn_a_hidden_states, L0_attn_a_rms_weight, L0_attn_a_in_proj_w,
           L0_attn_a_conv_w, L0_attn_a_conv_b, L0_attn_a_dt_bias, L0_attn_a_A_log,
           L0_attn_a_D, L0_attn_a_norm_w, L0_attn_a_out_proj_w,
           L0_attn_b_wq_codes, L0_attn_b_wq_scales, L0_attn_b_bq,
           L0_attn_b_wk_codes, L0_attn_b_wk_scales, L0_attn_b_bk,
           L0_attn_b_wv_codes, L0_attn_b_wv_scales, L0_attn_b_bv,
           L0_attn_b_wo_codes, L0_attn_b_wo_scales, L0_attn_b_bo,
           L0_ffn_rms_weight, L0_ffn_router_w, L0_ffn_w_gate, L0_ffn_w_up,
           L0_ffn_w_down, L0_ffn_sw_gate, L0_ffn_sw_up, L0_ffn_sw_down):
    f32 = np.float32
    A = lambda x: np.ascontiguousarray(x, dtype=f32)

    x = A(L0_attn_a_hidden_states).reshape(T, D)
    inproj = A(L0_attn_a_in_proj_w)
    convw_full = A(L0_attn_a_conv_w)
    convb_full = A(L0_attn_a_conv_b)
    wop_t = A(L0_attn_a_out_proj_w).T.copy()

    sc_rep = lambda s: np.repeat(A(s), 32, axis=0)
    wq_s, wk_s, wv_s = sc_rep(L0_attn_b_wq_scales), sc_rep(L0_attn_b_wk_scales), \
        sc_rep(L0_attn_b_wv_scales)
    wo_srep = sc_rep(L0_attn_b_wo_scales)

    identf = np.eye(128, dtype=f32)
    import ml_dtypes
    identb = np.eye(128, dtype=f32).astype(ml_dtypes.bfloat16)
    trimg = np.triu(np.ones((128, 128), dtype=f32))

    in_maps = []
    for c in range(NCORE):
        heads = list(range(c * HPC, (c + 1) * HPC))
        zrows = np.concatenate([np.arange(h * 64, h * 64 + 64) for h in heads])
        xrows = 4096 + zrows
        brows = np.arange(8192, 8256)
        crows = np.arange(8256, 8320)
        dtrows = 8320 + np.array(heads)
        rows = np.concatenate([zrows, xrows, brows, crows, dtrows])
        wmam_c = np.ascontiguousarray(inproj[rows, :].T)

        cw = np.zeros((6, 128, 4), dtype=f32)
        cb = np.zeros((6, 128), dtype=f32)
        cw[0:4] = convw_full[zrows].reshape(4, 128, 4)
        cb[0:4] = convb_full[zrows].reshape(4, 128)
        cw[4, 0:64] = convw_full[4096:4160]
        cb[4, 0:64] = convb_full[4096:4160]
        cw[5, 0:64] = convw_full[4160:4224]
        cb[5, 0:64] = convb_full[4160:4224]
        convw_c = np.ascontiguousarray(cw.transpose(1, 0, 2).reshape(128, 24))
        convb_c = np.ascontiguousarray(cb.transpose(1, 0).reshape(128, 6))

        fcols = slice(c * 256, (c + 1) * 256)
        m = {
            "x0": np.ascontiguousarray(x[c * TPC:(c + 1) * TPC]),
            "wmam": wmam_c,
            "convw": convw_c,
            "convb": convb_c,
            "dtbias": A(L0_attn_a_dt_bias)[heads].reshape(8, 1),
            "alog": A(L0_attn_a_A_log)[heads].reshape(8, 1),
            "dbc": np.ascontiguousarray(
                np.broadcast_to(A(L0_attn_a_D)[heads], (128, 8))),
            "rmsw1": np.ascontiguousarray(
                np.broadcast_to(A(L0_attn_a_rms_weight), (128, D))),
            "normw": np.ascontiguousarray(
                np.broadcast_to(A(L0_attn_a_norm_w), (128, DI))),
            "wop": wop_t,
            "ffnw": np.ascontiguousarray(
                np.broadcast_to(A(L0_ffn_rms_weight), (128, D))),
            "routw": A(L0_ffn_router_w),
            "wgate": A(L0_ffn_w_gate),
            "wup": A(L0_ffn_w_up),
            "wdown": A(L0_ffn_w_down),
            "swg": A(L0_ffn_sw_gate),
            "swu": A(L0_ffn_sw_up),
            "swd": A(L0_ffn_sw_down),
            "wqkv": np.ascontiguousarray(np.stack([
                A(L0_attn_b_wq_codes)[:, fcols],
                A(L0_attn_b_wk_codes)[:, fcols],
                A(L0_attn_b_wv_codes)[:, fcols]])),
            "wqkvs": np.ascontiguousarray(np.stack([
                wq_s[:, fcols], wk_s[:, fcols], wv_s[:, fcols]])),
            "bqk": np.ascontiguousarray(np.stack([
                A(L0_attn_b_bq)[fcols][0:128], A(L0_attn_b_bq)[fcols][128:256],
                A(L0_attn_b_bk)[fcols][0:128], A(L0_attn_b_bk)[fcols][128:256]],
                axis=1)),
            "bvb": np.ascontiguousarray(
                np.broadcast_to(A(L0_attn_b_bv)[fcols], (128, 256))),
            "woc": A(L0_attn_b_wo_codes),
            "wos": wo_srep,
            "bob": np.ascontiguousarray(
                np.broadcast_to(A(L0_attn_b_bo), (128, D))),
            "identf": identf,
            "cepsi": np.ascontiguousarray(np.broadcast_to(
                np.array([1e-10, 1e-11, 1e-5, 1.0], dtype=f32), (128, 4))),
            "identb": identb,
            "trim": trimg,
        }
        in_maps.append(m)

    return in_maps


def kernel(**inputs):
    global _NC_CACHE
    if _NC_CACHE is None:
        _NC_CACHE = build_nc()
    nc = _NC_CACHE
    in_maps = prep_inputs(**inputs)
    res = run_bass_kernel_spmd(nc, in_maps, core_ids=list(range(NCORE)))
    global _LAST_RES
    _LAST_RES = res
    o = np.concatenate([res.results[c]["o"] for c in range(NCORE)], axis=0)
    return o.reshape(B, L, D).astype(np.float32)


_LAST_RES = None


# version 02 (final)
# speedup vs baseline: 1.5444x; 1.5444x over previous
import numpy as np
import contextlib
from collections import defaultdict
import concourse.bass as bass
import concourse.mybir as mybir
from concourse.bass_utils import run_bass_kernel_spmd

F32 = mybir.dt.float32
BF16 = mybir.dt.bfloat16
AF = mybir.ActivationFunctionType
ALU = mybir.AluOpType
AX = mybir.AxisListType

NCORE = 8
B, L, D = 2, 4096, 2048
T = B * L
TPC = T // NCORE
DI, DS, NH, HD = 4096, 64, 64, 64
HPC = NH // NCORE
Q = 128
NCH = L // Q
ZC = DI // NCORE
E, FF, SF = 8, 1024, 1024
S128I = 1.0 / float(np.sqrt(128.0))
EPS_LN = 1e-5

DBG = []


class P:
    def __init__(self, nc, block, sems):
        self.nc, self.block = nc, block
        self.cnt = defaultdict(int)
        self.sem = sems

    def dma(self, eng, key, out, in_, **kw):
        if self.cnt[key] > 0:
            eng.wait_ge(self.sem[key], self.cnt[key])
        eng.dma_start(out=out, in_=in_, **kw).then_inc(self.sem[key], 16)
        self.cnt[key] += 16
        return self.cnt[key]

    def tick(self, instr, key, n=1):
        instr.then_inc(self.sem[key], n)
        self.cnt[key] += n
        return self.cnt[key]

    def wait(self, eng, key, val=None):
        v = self.cnt[key] if val is None else val
        if v > 0:
            eng.wait_ge(self.sem[key], v)


def build_nc():
    nc = bass.Bass("TRN2", target_bir_lowering=False)
    dt = nc.dram_tensor

    x0 = dt("x0", [TPC, D], F32, kind="ExternalInput").ap()
    wmam = dt("wmam", [D, 1160], F32, kind="ExternalInput").ap()
    convw = dt("convw", [128, 24], F32, kind="ExternalInput").ap()
    convb = dt("convb", [128, 6], F32, kind="ExternalInput").ap()
    dtbias = dt("dtbias", [8, 1], F32, kind="ExternalInput").ap()
    alog = dt("alog", [8, 1], F32, kind="ExternalInput").ap()
    dbc = dt("dbc", [128, 8], F32, kind="ExternalInput").ap()
    rmsw1 = dt("rmsw1", [128, D], F32, kind="ExternalInput").ap()
    normw = dt("normw", [128, DI], F32, kind="ExternalInput").ap()
    wop = dt("wop", [DI, D], F32, kind="ExternalInput").ap()
    ffnw = dt("ffnw", [128, D], F32, kind="ExternalInput").ap()
    routw = dt("routw", [D, E], F32, kind="ExternalInput").ap()
    wgate = dt("wgate", [E, D, FF], F32, kind="ExternalInput").ap()
    wup = dt("wup", [E, D, FF], F32, kind="ExternalInput").ap()
    wdown = dt("wdown", [E, FF, D], F32, kind="ExternalInput").ap()
    swg = dt("swg", [D, SF], F32, kind="ExternalInput").ap()
    swu = dt("swu", [D, SF], F32, kind="ExternalInput").ap()
    swd = dt("swd", [SF, D], F32, kind="ExternalInput").ap()
    wqkv = dt("wqkv", [3, D, 256], F32, kind="ExternalInput").ap()
    wqkvs = dt("wqkvs", [3, D, 256], F32, kind="ExternalInput").ap()
    bqk = dt("bqk", [128, 4], F32, kind="ExternalInput").ap()
    bvb = dt("bvb", [128, 256], F32, kind="ExternalInput").ap()
    woc = dt("woc", [D, D], F32, kind="ExternalInput").ap()
    wos = dt("wos", [D, D], F32, kind="ExternalInput").ap()
    bob = dt("bob", [128, D], F32, kind="ExternalInput").ap()
    identf = dt("identf", [128, 128], F32, kind="ExternalInput").ap()
    identb = dt("identb", [128, 128], BF16, kind="ExternalInput").ap()
    trim = dt("trim", [128, 128], F32, kind="ExternalInput").ap()
    cepsi = dt("cepsi", [128, 4], F32, kind="ExternalInput").ap()
    out = dt("o", [TPC, D], F32, kind="ExternalOutput").ap()

    u_sl = dt("u_sl", [TPC, D], BF16)
    u_full = dt("u_full", [T, D], BF16)
    xbcd = dt("xbcd", [6, 128, T], BF16)
    xbcc = dt("xbcc", [6, 128, T], BF16)
    siluz = dt("siluz", [T, ZC], BF16)
    ellh = dt("ellh", [8, T], F32)
    yg_sl = dt("yg_sl", [T, ZC], BF16)
    yg_full = dt("yg_full", [NCORE, T, ZC], BF16)
    yn_h = dt("yn_h", [TPC, DI], BF16)
    x1h = dt("x1h", [TPC, D], F32)
    hh = dt("hh", [TPC, D], BF16)
    x2h = dt("x2h", [TPC, D], F32)
    u2_sl = dt("u2_sl", [TPC, D], BF16)
    u2_full = dt("u2_full", [T, D], BF16)
    ao_sl = dt("ao_sl", [256, T], BF16)
    ao_full = dt("ao_full", [NCORE, 256, T], BF16)
    x3h = dt("x3h", [TPC, D], F32)
    ihbm = {"u_sl": u_sl, "siluz": siluz, "xbcc": xbcc, "ellh": ellh,
            "yg_sl": yg_sl, "yn_h": yn_h, "x1h": x1h, "hh": hh, "x2h": x2h,
            "u2_sl": u2_sl, "ao_sl": ao_sl, "x3h": x3h}
    dbg_outs = {}
    for name in DBG:
        src = ihbm[name]
        sh = list(src.shape) if not hasattr(src, "ap") else list(src.ap().shape)
        dbg_outs[name] = (dt("dbg_" + name, sh, src.ap().dtype,
                             kind="ExternalOutput").ap(), src)

    STK = contextlib.ExitStack()
    with STK:
        EC = STK.enter_context
        c_idf = EC(nc.sbuf_tensor("c_idf", [128, 128], F32))
        c_idb = EC(nc.sbuf_tensor("c_idb", [128, 128], BF16))
        c_tri = EC(nc.sbuf_tensor("c_tri", [128, 128], F32))
        c_onesb = EC(nc.sbuf_tensor("c_onesb", [128, 1], BF16))
        c_onesf = EC(nc.sbuf_tensor("c_onesf", [1, 128], F32))
        c_dbc = EC(nc.sbuf_tensor("c_dbc", [128, 8], F32))
        c_convw = EC(nc.sbuf_tensor("c_convw", [128, 24], F32))
        c_convb = EC(nc.sbuf_tensor("c_convb", [128, 6], F32))
        c_small = EC(nc.sbuf_tensor("c_small", [8, 8], F32))
        c_eps = EC(nc.sbuf_tensor("c_eps", [128, 4], F32))
        st_m = EC(nc.sbuf_tensor("st_m", [128, 8], F32))
        st_a = EC(nc.sbuf_tensor("st_a", [128, 8], F32))
        st_b = EC(nc.sbuf_tensor("st_b", [128, 8], F32))
        st_c = EC(nc.sbuf_tensor("st_c", [128, 8], F32))
        st_t = EC(nc.sbuf_tensor("st_t", [128, 8], F32))
        tp1a = EC(nc.sbuf_tensor("tp1a", [128, 1], F32))
        tp1b = EC(nc.sbuf_tensor("tp1b", [128, 1], F32))
        wtok = EC(nc.sbuf_tensor("wtok", [128, 64], F32))
        s_qs = EC(nc.semaphore("s_qs"))
        s_qg = EC(nc.semaphore("s_qg"))
        s_te = EC(nc.semaphore("s_te"))
        s_ve = EC(nc.semaphore("s_ve"))
        s_se = EC(nc.semaphore("s_se"))
        s_cc = EC(nc.semaphore("s_cc"))
        block = EC(nc.Block())
        p = P(nc, block, {"qs": s_qs, "qg": s_qg, "te": s_te, "ve": s_ve,
                          "se": s_se, "cc": s_cc})

        def SEC(name):
            def deco(f):
                if name in ("vector", "scalar"):
                    def wrapped(eng, _f=f):
                        eng.drain()
                        _f(eng)
                    getattr(block, name)(wrapped)
                else:
                    getattr(block, name)(f)
            return deco

        @SEC("gpsimd")
        def _(g):
            p.dma(g, "qg", c_idf[:, :], identf)
            p.dma(g, "qg", c_idb[:, :], identb)
            p.dma(g, "qg", c_tri[:, :], trim)
            p.dma(g, "qg", c_dbc[:, :], dbc)
            p.dma(g, "qg", c_convw[:, :], convw)
            p.dma(g, "qg", c_convb[:, :], convb)
            p.dma(g, "qg", c_small[0:8, 0:1], dtbias)
            p.dma(g, "qg", c_small[0:8, 1:2], alog)
            p.dma(g, "qg", c_eps[:, :], cepsi)
            p.tick(g.memset(c_onesb[:, :], 1.0), "cc")
            p.tick(g.memset(c_onesf[:, :], 1.0), "cc")
        consts_qg = p.cnt["qg"]

        @SEC("scalar")
        def _(s):
            p.wait(s, "qg", consts_qg)
            p.tick(s.activation(c_small[0:8, 3:4], c_small[0:8, 1:2], AF.Exp), "se")

        @SEC("vector")
        def _(v):
            p.wait(v, "se")
            p.tick(v.tensor_scalar_mul(c_small[0:8, 2:3], c_small[0:8, 3:4], -1.0), "ve")

        def emit_norm(x_hbm, w_bcast, eps2, dst_hbm, xa, scr, keep_stats):
            for tt in range(8):
                @SEC("gpsimd")
                def _(g, tt=tt):
                    p.wait(g, "ve")
                    p.dma(g, "qg", xa, x_hbm[tt * 128:(tt + 1) * 128, :])
                ld = p.cnt["qg"]

                @SEC("scalar")
                def _(s, tt=tt, ld=ld):
                    p.wait(s, "qg", ld)
                    p.wait(s, "qs")
                    p.tick(s.activation(scr, xa, AF.Square,
                                        accum_out=st_t[:, tt:tt + 1]), "se")
                sq = p.cnt["se"]

                @SEC("vector")
                def _(v, tt=tt, ld=ld, sq=sq):
                    p.wait(v, "qg", ld)
                    p.tick(v.reduce_sum(st_m[:, tt:tt + 1], xa, axis=AX.X), "ve")
                    v.drain()
                    p.tick(v.tensor_scalar_mul(st_m[:, tt:tt + 1],
                                               st_m[:, tt:tt + 1], 1.0 / D), "ve")
                    p.wait(v, "se", sq)
                    p.tick(v.tensor_scalar_mul(st_t[:, tt:tt + 1],
                                               st_t[:, tt:tt + 1], 1.0 / D), "ve")
                    v.drain()
                    p.tick(v.tensor_mul(st_c[:, tt:tt + 1], st_m[:, tt:tt + 1],
                                        st_m[:, tt:tt + 1]), "ve")
                    v.drain()
                    p.tick(v.tensor_sub(st_t[:, tt:tt + 1], st_t[:, tt:tt + 1],
                                        st_c[:, tt:tt + 1]), "ve")
                vd = p.cnt["ve"]

                @SEC("scalar")
                def _(s, tt=tt, vd=vd):
                    p.wait(s, "ve", vd)
                    if eps2 is not None:
                        bcol = 0 if eps2 == 1e-5 else 1
                        p.tick(s.activation(st_c[:, tt:tt + 1], st_t[:, tt:tt + 1],
                                            AF.Ln, scale=1.0 + eps2,
                                            bias=c_eps[:, bcol:bcol + 1]), "se")
                    else:
                        p.tick(s.activation(st_c[:, tt:tt + 1], st_t[:, tt:tt + 1],
                                            AF.Ln, bias=c_eps[:, 2:3]), "se")
                    s.drain()
                    p.tick(s.activation(st_c[:, tt:tt + 1], st_c[:, tt:tt + 1],
                                        AF.Exp, scale=-0.5), "se")
                    if keep_stats:
                        p.tick(s.activation(st_a[:, tt:tt + 1], st_t[:, tt:tt + 1],
                                            AF.Ln, bias=c_eps[:, 2:3]), "se")
                        s.drain()
                        p.tick(s.activation(st_a[:, tt:tt + 1], st_a[:, tt:tt + 1],
                                            AF.Exp, scale=-0.5), "se")
                sd = p.cnt["se"]

                @SEC("vector")
                def _(v, tt=tt, sd=sd):
                    p.wait(v, "se", sd)
                    if keep_stats:
                        p.tick(v.tensor_mul(st_b[:, tt:tt + 1], st_m[:, tt:tt + 1],
                                            st_a[:, tt:tt + 1]), "ve")
                        v.drain()
                        p.tick(v.tensor_scalar_add(st_a[:, tt:tt + 1],
                                                   st_a[:, tt:tt + 1], 1.0), "ve")
                    p.tick(v.tensor_scalar(scr, xa, st_m[:, tt:tt + 1],
                                           st_c[:, tt:tt + 1], ALU.subtract,
                                           ALU.mult), "ve")
                    if w_bcast is not None:
                        v.drain()
                        p.tick(v.tensor_mul(scr, scr, w_bcast), "ve")
                vd2 = p.cnt["ve"]

                @SEC("sync")
                def _(sy, tt=tt, vd2=vd2):
                    p.wait(sy, "ve", vd2)
                    p.dma(sy, "qs", dst_hbm[tt * 128:(tt + 1) * 128, :], scr)

        def emit_ag(ins_t, outs_t):
            @SEC("gpsimd")
            def _(g):
                p.wait(g, "qs")
                p.wait(g, "qg")
                g.collective_compute(
                    "AllGather", ALU.bypass, replica_groups=[list(range(NCORE))],
                    ins=[ins_t.ap().opt()], outs=[outs_t.ap().opt()],
                ).then_inc(p.sem["cc"], 1)
                p.cnt["cc"] += 1

        def join_all():
            for nm in ("tensor", "scalar", "gpsimd", "sync", "vector"):
                @SEC(nm)
                def _(e):
                    for k in ("te", "ve", "se", "qs", "qg", "cc"):
                        p.wait(e, k)

        with (
            nc.sbuf_tensor("c_rmsw1", [128, D], BF16) as c_rmsw1,
            nc.sbuf_tensor("p1x", [128, D], F32) as p1x,
            nc.sbuf_tensor("p1s", [128, D], BF16) as p1s,
        ):
            @SEC("gpsimd")
            def _(g):
                p.dma(g, "qg", c_rmsw1[:, :], rmsw1)
            emit_norm(x0, c_rmsw1[:, :], 1e-5, u_sl.ap(), p1x[:, :], p1s[:, :], True)
            emit_ag(u_sl, u_full)
            join_all()

        with (
            nc.sbuf_tensor("r_dt", [8, T], F32) as r_dt,
            nc.sbuf_tensor("r_el", [8, T], F32) as r_el,
        ):
            with (
                nc.sbuf_tensor("m_w", [128, 16 * 1160], BF16) as m_w,
                nc.sbuf_tensor("m_ut", [128, 2 * 16 * 512], BF16) as m_ut,
                nc.sbuf_tensor("m_st", [128, 512], BF16) as m_st,
                nc.sbuf_tensor("m_xs", [128, 512], BF16) as m_xs,
                nc.sbuf_tensor("r_wk", [8, T], F32) as r_wk,
                nc.psum_tensor("pz", [128, 512], F32) as pz,
                nc.psum_tensor("px", [128, 512], F32) as px,
                nc.psum_tensor("px2", [128, 512], F32) as px2,
            ):
                @SEC("gpsimd")
                def _(g):
                    for k in range(16):
                        p.dma(g, "qg", m_w[:, k * 1160:(k + 1) * 1160],
                              wmam[k * 128:(k + 1) * 128, :])
                wld = p.cnt["qg"]

                for nchk in range(16):
                    par = nchk % 2
                    t0 = nchk * 512

                    @SEC("sync")
                    def _(sy, par=par, t0=t0):
                        p.wait(sy, "te")
                        for k in range(16):
                            p.dma(sy, "qs",
                                  m_ut[:, (par * 16 + k) * 512:(par * 16 + k + 1) * 512],
                                  u_full.ap()[t0:t0 + 512, k * 128:(k + 1) * 128],
                                  transpose=True)
                    utld = p.cnt["qs"]

                    def UTK(k, par=par):
                        return m_ut[:, (par * 16 + k) * 512:(par * 16 + k + 1) * 512]

                    for ts_ in range(4):
                        @SEC("tensor")
                        def _(t, ts_=ts_, utld=utld, nchk=nchk, UTK=UTK):
                            p.wait(t, "qs", utld)
                            if nchk == 0:
                                p.wait(t, "qg", wld)
                            p.wait(t, "se")
                            for k in range(16):
                                mm = t.matmul(pz[:, :],
                                              UTK(k)[:, ts_ * 128:(ts_ + 1) * 128],
                                              m_w[:, k * 1160:k * 1160 + 512],
                                              start=(k == 0), stop=(k == 15))
                            p.tick(mm, "te")
                        tev = p.cnt["te"]

                        @SEC("scalar")
                        def _(s, tev=tev):
                            p.wait(s, "te", tev)
                            p.wait(s, "qs")
                            p.tick(s.activation(m_st[:, :], pz[:, :], AF.Silu), "se")
                        sev = p.cnt["se"]

                        @SEC("sync")
                        def _(sy, t0=t0, ts_=ts_, sev=sev):
                            p.wait(sy, "se", sev)
                            p.dma(sy, "qs",
                                  siluz.ap()[t0 + ts_ * 128:t0 + (ts_ + 1) * 128, :],
                                  m_st[:, :])

                    for m in range(7):
                        mw0 = [512, 640, 768, 896, 1024, 1088, 1152][m]
                        mcols = [128, 128, 128, 128, 64, 64, 8][m]

                        @SEC("tensor")
                        def _(t, m=m, mw0=mw0, mcols=mcols, UTK=UTK):
                            p.wait(t, "ve")
                            pb = px if (m % 2 == 0) else px2
                            for k in range(16):
                                mm = t.matmul(pb[0:mcols, :],
                                              m_w[:, k * 1160 + mw0:k * 1160 + mw0 + mcols],
                                              UTK(k), start=(k == 0), stop=(k == 15))
                            p.tick(mm, "te")
                        tev = p.cnt["te"]

                        if m < 6:
                            @SEC("vector")
                            def _(v, m=m, tev=tev):
                                p.wait(v, "te", tev)
                                p.wait(v, "qs")
                                pb = px if (m % 2 == 0) else px2
                                p.tick(v.tensor_copy(m_xs[:, :], pb[:, :]), "ve")
                            vev = p.cnt["ve"]

                            @SEC("sync")
                            def _(sy, m=m, t0=t0, vev=vev):
                                p.wait(sy, "ve", vev)
                                p.dma(sy, "qs", xbcd.ap()[m, :, t0:t0 + 512],
                                      m_xs[:, :])
                        else:
                            @SEC("vector")
                            def _(v, t0=t0, tev=tev):
                                p.wait(v, "te", tev)
                                p.tick(v.tensor_copy(r_dt[:, t0:t0 + 512],
                                                     px[0:8, :]), "ve")

                @SEC("vector")
                def _(v):
                    p.tick(v.tensor_scalar(r_wk[:, :], r_dt[:, :],
                                           c_small[0:8, 0:1], 0.0,
                                           ALU.add, ALU.max), "ve")
                relu_d = p.cnt["ve"]

                @SEC("scalar")
                def _(s, relu_d=relu_d):
                    p.wait(s, "ve", relu_d)
                    p.tick(s.activation(r_el[:, :], r_dt[:, :], AF.Abs,
                                        bias=c_small[0:8, 0:1]), "se")
                    s.drain()
                    p.tick(s.activation(r_el[:, :], r_el[:, :], AF.Exp,
                                        scale=-1.0), "se")
                    s.drain()
                    p.tick(s.activation(r_el[:, :], r_el[:, :], AF.Ln,
                                        bias=c_eps[0:8, 3:4]), "se")
                sp_d = p.cnt["se"]

                @SEC("vector")
                def _(v, sp_d=sp_d):
                    p.wait(v, "se", sp_d)
                    p.tick(v.tensor_add(r_dt[:, :], r_wk[:, :], r_el[:, :]), "ve")
                    v.drain()
                    p.tick(v.tensor_scalar_mul(r_wk[:, :], r_dt[:, :],
                                               c_small[0:8, 2:3]), "ve")
                    v.drain()
                    for b_ in range(B):
                        for c_ in range(NCH):
                            t0 = b_ * L + c_ * Q
                            p.tick(v.tensor_tensor_scan(
                                r_el[:, t0:t0 + Q], r_wk[:, t0:t0 + Q],
                                r_wk[:, t0:t0 + Q], 0.0, ALU.add, ALU.bypass), "ve")
                ell_d = p.cnt["ve"]

                @SEC("sync")
                def _(sy, ell_d=ell_d):
                    p.wait(sy, "ve", ell_d)
                    p.dma(sy, "qs", ellh.ap(), r_el[:, :])
                join_all()

            with (
                nc.sbuf_tensor("cvx", [128, T], BF16) as cvx,
                nc.sbuf_tensor("cva", [128, T], F32) as cva,
                nc.sbuf_tensor("cvo", [128, T], BF16) as cvo,
            ):
                for m in range(6):
                    @SEC("gpsimd")
                    def _(g, m=m):
                        p.wait(g, "ve")
                        p.dma(g, "qg", cvx[:, :], xbcd.ap()[m, :, :])
                    ld = p.cnt["qg"]

                    @SEC("vector")
                    def _(v, m=m, ld=ld):
                        p.wait(v, "qg", ld)
                        p.wait(v, "se")
                        for b_ in range(B):
                            s0, s1 = b_ * L, (b_ + 1) * L
                            p.tick(v.tensor_scalar_mul(
                                cva[:, s0:s1], cvx[:, s0:s1],
                                c_convw[:, m * 4 + 3:m * 4 + 4]), "ve")
                            for j in range(3):
                                sh = 3 - j
                                v.drain()
                                p.tick(v.scalar_tensor_tensor(
                                    cva[:, s0 + sh:s1], cvx[:, s0:s1 - sh],
                                    c_convw[:, m * 4 + j:m * 4 + j + 1],
                                    cva[:, s0 + sh:s1], ALU.mult, ALU.add), "ve")
                    cvd = p.cnt["ve"]

                    @SEC("scalar")
                    def _(s, m=m, cvd=cvd):
                        p.wait(s, "ve", cvd)
                        p.wait(s, "qs")
                        p.tick(s.activation(cvo[:, :], cva[:, :], AF.Silu,
                                            bias=c_convb[:, m:m + 1]), "se")
                    sv = p.cnt["se"]

                    @SEC("sync")
                    def _(sy, m=m, sv=sv):
                        p.wait(sy, "se", sv)
                        p.dma(sy, "qs", xbcc.ap()[m, :, :], cvo[:, :])
                join_all()

            SSTK = contextlib.ExitStack()
            with SSTK:
                EC2 = SSTK.enter_context
                sxch = EC2(nc.sbuf_tensor("sxch", [128, 2 * 768], BF16))
                ssz = EC2(nc.sbuf_tensor("ssz", [128, 2 * 512], BF16))
                selb = EC2(nc.sbuf_tensor("selb", [128, 2 * 1024], F32))
                sexpo = EC2(nc.sbuf_tensor("sexpo", [128, 1024], F32))
                seall = EC2(nc.sbuf_tensor("seall", [128, 1024], BF16))
                seb = EC2(nc.sbuf_tensor("seb", [64, 1024], BF16))
                smt = EC2(nc.sbuf_tensor("smt", [128, 128], BF16))
                swt = EC2(nc.sbuf_tensor("swt", [128, 1024], BF16))
                swts = EC2(nc.sbuf_tensor("swts", [64, 1024], BF16))
                sxtok = EC2(nc.sbuf_tensor("sxtok", [128, 512], BF16))
                sbct = EC2(nc.sbuf_tensor("sbct", [128, 64], BF16))
                sdet = EC2(nc.sbuf_tensor("sdet", [128, 16], F32))
                sds = EC2(nc.sbuf_tensor("sds", [128, 8], F32))
                seq = EC2(nc.sbuf_tensor("seq", [64, 8], F32))
                sxs = EC2(nc.sbuf_tensor("sxs", [128, 512], BF16))
                sei = EC2(nc.sbuf_tensor("sei", [64, 512], BF16))
                ss = EC2(nc.sbuf_tensor("ss", [64, 512], BF16))
                syg = EC2(nc.sbuf_tensor("syg", [128, 512], BF16))
                spa = EC2(nc.psum_tensor("spa", [128, 512], F32))
                spy = EC2(nc.psum_tensor("spy", [128, 512], F32))
                sps = EC2(nc.psum_tensor("sps", [64, 512], F32))
                spt = EC2(nc.psum_tensor("spt", [128, 1024], BF16))
                for b_ in range(B):
                    @SEC("vector")
                    def _(v):
                        p.tick(v.memset(ss[:, :], 0.0), "ve")
                    for c_ in range(NCH):
                        it = b_ * NCH + c_
                        t0 = b_ * L + c_ * Q
                        par = it % 2
                        XCH = sxch[:, par * 768:par * 768 + 768]
                        ELB = selb[:, par * 1024:(par + 1) * 1024]
                        SZ = ssz[:, par * 512:(par + 1) * 512]

                        @SEC("gpsimd")
                        def _(g, t0=t0, par=par, XCH=XCH, ELB=ELB):
                            p.wait(g, "ve")
                            p.dma(g, "qg", XCH,
                                  xbcc.ap().rearrange("v p t -> p v t")[:, :, t0:t0 + 128])
                            p.dma(g, "qg", ELB,
                                  bass.AP(ellh.ap().tensor, t0,
                                          [[0, 128], [T, 8], [1, 128]]))
                        ldx = p.cnt["qg"]

                        @SEC("sync")
                        def _(sy, t0=t0, SZ=SZ):
                            p.wait(sy, "ve")
                            p.dma(sy, "qs", SZ, siluz.ap()[t0:t0 + 128, :])
                        ldz = p.cnt["qs"]

                        @SEC("tensor")
                        def _(t, t0=t0, ldx=ldx, XCH=XCH):
                            p.wait(t, "qg", ldx)
                            p.wait(t, "ve")
                            t.transpose(spa[:, 0:8], r_el[:, t0:t0 + 128], c_idf[0:8, 0:8])
                            t.transpose(spa[:, 8:16], r_dt[:, t0:t0 + 128], c_idf[0:8, 0:8])
                            for m in range(4):
                                t.transpose(spt[:, m * 128:(m + 1) * 128],
                                            XCH[:, m * 128:(m + 1) * 128], c_idb[:, :])
                            t.transpose(spt[:, 512:640], XCH[:, 512:640], c_idb[:, :])
                            mm = t.matmul(spa[:, 128:256], XCH[0:64, 512:640],
                                          XCH[0:64, 640:768], start=True, stop=True)
                            p.tick(mm, "te")
                        te1 = p.cnt["te"]

                        @SEC("vector")
                        def _(v, te1=te1, ELB=ELB):
                            p.wait(v, "te", te1)
                            p.tick(v.tensor_copy(sdet[:, :], spa[:, 0:16]), "ve")
                            p.tick(v.tensor_copy(sxtok[:, :], spt[:, 0:512]), "ve")
                            p.tick(v.tensor_copy(sbct[:, :], spt[:, 512:576]), "ve")
                            p.tick(v.tensor_mul(smt[:, :], spa[:, 128:256],
                                                c_tri[:, :]), "ve")
                            v.drain()
                            for h in range(8):
                                p.tick(v.tensor_scalar(
                                    sexpo[:, h * 128:(h + 1) * 128],
                                    ELB[:, h * 128:(h + 1) * 128],
                                    sdet[:, h:h + 1], 0.0,
                                    ALU.subtract, ALU.min), "ve")
                        ve1 = p.cnt["ve"]

                        @SEC("scalar")
                        def _(s, ve1=ve1, ELB=ELB):
                            p.wait(s, "ve", ve1)
                            p.tick(s.activation(seall[:, :], sexpo[:, :], AF.Exp), "se")
                            p.tick(s.activation(seb[:, :], ELB[0:64, :], AF.Exp), "se")
                        se1 = p.cnt["se"]

                        @SEC("vector")
                        def _(v, se1=se1, XCH=XCH):
                            p.wait(v, "se", se1)
                            p.tick(v.tensor_copy(
                                seq[:, :],
                                seb[0:64, :].rearrange("p (h q) -> p h q", q=128)[:, :, 127:128]), "ve")
                            for h in range(8):
                                Eh = seall[:, h * 128:(h + 1) * 128]
                                p.tick(v.scalar_tensor_tensor(
                                    swt[:, h * 128:(h + 1) * 128], Eh,
                                    sdet[:, 8 + h:9 + h], smt[:, :],
                                    ALU.mult, ALU.mult), "ve")
                                p.tick(v.tensor_mul(
                                    sds[:, h:h + 1], sdet[:, 8 + h:9 + h],
                                    seall[:, h * 128 + 127:h * 128 + 128]), "ve")
                            v.drain()
                            for h in range(8):
                                p.tick(v.scalar_tensor_tensor(
                                    swt[:, h * 128:(h + 1) * 128], c_idb[:, :],
                                    c_dbc[:, h:h + 1], swt[:, h * 128:(h + 1) * 128],
                                    ALU.mult, ALU.add), "ve")
                                p.tick(v.tensor_mul(
                                    swts[0:64, h * 128:(h + 1) * 128],
                                    XCH[0:64, 640:768],
                                    seb[0:64, h * 128:(h + 1) * 128]), "ve")
                                p.tick(v.tensor_scalar_mul(
                                    sxs[:, h * 64:(h + 1) * 64],
                                    sxtok[:, h * 64:(h + 1) * 64],
                                    sds[:, h:h + 1]), "ve")
                                p.tick(v.tensor_scalar_mul(
                                    sei[0:64, h * 64:(h + 1) * 64],
                                    c_idb[0:64, 0:64],
                                    seq[:, h:h + 1]), "ve")
                        ve2 = p.cnt["ve"]

                        @SEC("tensor")
                        def _(t, ve2=ve2):
                            p.wait(t, "ve", ve2)
                            for h in range(8):
                                t.matmul(spy[:, h * 64:(h + 1) * 64],
                                         swt[:, h * 128:(h + 1) * 128],
                                         sxtok[:, h * 64:(h + 1) * 64],
                                         start=True, stop=False)
                                mm = t.matmul(spy[:, h * 64:(h + 1) * 64],
                                              swts[0:64, h * 128:(h + 1) * 128],
                                              ss[:, h * 64:(h + 1) * 64],
                                              start=False, stop=True)
                            p.tick(mm, "te")
                        tey = p.cnt["te"]

                        @SEC("tensor")
                        def _(t):
                            for h in range(8):
                                t.matmul(sps[:, h * 64:(h + 1) * 64],
                                         sbct[:, :],
                                         sxs[:, h * 64:(h + 1) * 64],
                                         start=True, stop=False)
                                mm = t.matmul(sps[:, h * 64:(h + 1) * 64],
                                              sei[0:64, h * 64:(h + 1) * 64],
                                              ss[:, h * 64:(h + 1) * 64],
                                              start=False, stop=True)
                            p.tick(mm, "te")
                        tes = p.cnt["te"]

                        @SEC("vector")
                        def _(v, tey=tey, tes=tes, ldz=ldz, SZ=SZ):
                            p.wait(v, "te", tey)
                            p.wait(v, "qs", ldz)
                            p.tick(v.tensor_mul(syg[:, :], spy[:, :], SZ), "ve")
                            p.wait(v, "te", tes)
                            p.tick(v.tensor_copy(ss[:, :], sps[:, :]), "ve")
                        vey = p.cnt["ve"]

                        @SEC("sync")
                        def _(sy, t0=t0, vey=vey):
                            p.wait(sy, "ve", vey)
                            p.dma(sy, "qs", yg_sl.ap()[t0:t0 + 128, :], syg[:, :])
                join_all()

        emit_ag(yg_sl, yg_full)

        with (
            nc.sbuf_tensor("c_nw", [128, DI], BF16) as c_nw,
            nc.sbuf_tensor("fyg", [128, 8 * DI], BF16) as fyg,
            nc.sbuf_tensor("fsc", [128, DI], BF16) as fsc,
        ):
            @SEC("gpsimd")
            def _(g):
                p.dma(g, "qg", c_nw[:, :], normw)
                p.wait(g, "cc")
                fygv = fyg[:, :].rearrange("p (tt c) -> p tt c", c=DI)
                qg_base = p.cnt["qg"]
                for arm in nc.switch_core_id(engines=[g], n=128):
                    r = arm.logical % NCORE
                    for src in range(NCORE):
                        g.wait_ge(p.sem["qg"], qg_base + src * 16)
                        g.dma_start(
                            out=fygv[:, :, src * ZC:(src + 1) * ZC],
                            in_=yg_full.ap()[src, r * TPC:(r + 1) * TPC, :]
                            .rearrange("(tt pp) z -> pp tt z", pp=128),
                        ).then_inc(p.sem["qg"], 16)
                p.cnt["qg"] += 8 * 16
            gath = p.cnt["qg"]

            for tt in range(8):
                @SEC("scalar")
                def _(s, tt=tt, gath=gath):
                    p.wait(s, "qg", gath)
                    p.wait(s, "qs")
                    p.tick(s.activation(fsc[:, :], fyg[:, tt * DI:(tt + 1) * DI],
                                        AF.Square, accum_out=st_t[:, tt:tt + 1]), "se")
                    s.drain()
                    p.tick(s.activation(st_c[:, tt:tt + 1], st_t[:, tt:tt + 1],
                                        AF.Ln, scale=1.0 / DI,
                                        bias=c_eps[:, 2:3]), "se")
                    s.drain()
                    p.tick(s.activation(st_c[:, tt:tt + 1], st_c[:, tt:tt + 1],
                                        AF.Exp, scale=-0.5), "se")
                sev = p.cnt["se"]

                @SEC("vector")
                def _(v, tt=tt, sev=sev):
                    p.wait(v, "se", sev)
                    p.tick(v.tensor_scalar_mul(fsc[:, :], fyg[:, tt * DI:(tt + 1) * DI],
                                               st_c[:, tt:tt + 1]), "ve")
                    v.drain()
                    p.tick(v.tensor_mul(fsc[:, :], fsc[:, :], c_nw[:, :]), "ve")
                vev = p.cnt["ve"]

                @SEC("sync")
                def _(sy, tt=tt, vev=vev):
                    p.wait(sy, "ve", vev)
                    p.dma(sy, "qs", yn_h.ap()[tt * 128:(tt + 1) * 128, :], fsc[:, :])
            ynst = p.cnt["qs"]
            join_all()

        with (
            nc.sbuf_tensor("fw", [128, 32 * 2048], BF16) as fw,
            nc.sbuf_tensor("fyt", [128, 32 * 128], BF16) as fyt,
            nc.sbuf_tensor("fx0", [128, D], F32) as fx0,
            nc.sbuf_tensor("fxr", [128, D], F32) as fxr,
            nc.sbuf_tensor("fx1", [128, D], F32) as fx1,
            nc.psum_tensor("fp", [128, 512], F32) as fp,
            nc.psum_tensor("fp2", [128, 512], F32) as fp2,
        ):
            @SEC("gpsimd")
            def _(g):
                for k in range(32):
                    p.dma(g, "qg", fw[:, k * 2048:(k + 1) * 2048],
                          wop[k * 128:(k + 1) * 128, :])
            wld = p.cnt["qg"]

            for tt in range(8):
                @SEC("sync")
                def _(sy, tt=tt):
                    p.wait(sy, "te")
                    for k in range(32):
                        p.dma(sy, "qs", fyt[:, k * 128:(k + 1) * 128],
                              yn_h.ap()[tt * 128:(tt + 1) * 128,
                                        k * 128:(k + 1) * 128],
                              transpose=True)
                ytld = p.cnt["qs"]

                @SEC("gpsimd")
                def _(g, tt=tt):
                    p.wait(g, "ve")
                    p.dma(g, "qg", fx0[:, :], x0[tt * 128:(tt + 1) * 128, :])
                x0ld = p.cnt["qg"]

                @SEC("vector")
                def _(v, tt=tt, x0ld=x0ld):
                    p.wait(v, "qg", x0ld)
                    p.tick(v.tensor_scalar(fxr[:, :], fx0[:, :], st_a[:, tt:tt + 1],
                                           st_b[:, tt:tt + 1], ALU.mult,
                                           ALU.subtract), "ve")
                xrd = p.cnt["ve"]

                for dm in range(4):
                    pb = fp if dm % 2 == 0 else fp2

                    @SEC("tensor")
                    def _(t, tt=tt, dm=dm, pb=pb, ytld=ytld):
                        p.wait(t, "qs", ytld)
                        if tt == 0 and dm == 0:
                            p.wait(t, "qg", wld)
                        p.wait(t, "ve")
                        for k in range(32):
                            mm = t.matmul(pb[:, :], fyt[:, k * 128:(k + 1) * 128],
                                          fw[:, k * 2048 + dm * 512:k * 2048 + dm * 512 + 512],
                                          start=(k == 0), stop=(k == 31))
                        p.tick(mm, "te")
                    tev = p.cnt["te"]

                    @SEC("vector")
                    def _(v, tt=tt, dm=dm, pb=pb, tev=tev, xrd=xrd):
                        p.wait(v, "te", tev)
                        p.wait(v, "ve", xrd)
                        if dm == 0:
                            p.wait(v, "qs")
                        p.tick(v.scalar_tensor_tensor(
                            fx1[:, dm * 512:(dm + 1) * 512], pb[:, :], 1.0,
                            fxr[:, dm * 512:(dm + 1) * 512], ALU.mult, ALU.add), "ve")
                vd = p.cnt["ve"]

                @SEC("sync")
                def _(sy, tt=tt, vd=vd):
                    p.wait(sy, "ve", vd)
                    p.dma(sy, "qs", x1h.ap()[tt * 128:(tt + 1) * 128, :], fx1[:, :])
            join_all()

        def emit_ffn(x_in, x_out, idx):
            with (
                nc.sbuf_tensor("c_fw%d" % idx, [128, D], BF16) as c_fw,
                nc.sbuf_tensor("q_x%d" % idx, [128, D], F32) as q_x,
                nc.sbuf_tensor("q_s%d" % idx, [128, D], BF16) as q_s,
            ):
                @SEC("gpsimd")
                def _(g):
                    p.dma(g, "qg", c_fw[:, :], ffnw)
                emit_norm(x_in, c_fw[:, :], 1e-6, hh.ap(), q_x[:, :], q_s[:, :], False)
                join_all()

            FSTK = contextlib.ExitStack()
            with FSTK:
                EC3 = FSTK.enter_context
                f_ht = EC3(nc.sbuf_tensor("f_ht%d" % idx, [128, 16 * 1024], BF16))
                f_rw = EC3(nc.sbuf_tensor("f_rw%d" % idx, [128, 16 * 8], BF16))
                f_gu = EC3(nc.sbuf_tensor("f_gu%d" % idx, [128, 2 * 2 * 16 * 256], BF16))
                f_wd = EC3(nc.sbuf_tensor("f_wd%d" % idx, [128, 8 * 2048], BF16))
                f_act = EC3(nc.sbuf_tensor("f_act%d" % idx, [128, 8 * 1024], BF16))
                f_sil = EC3(nc.sbuf_tensor("f_sil%d" % idx, [128, 2 * 512], BF16))
                f_moe = EC3(nc.sbuf_tensor("f_moe%d" % idx, [128, 8 * 2048], F32))
                f_p8 = EC3(nc.sbuf_tensor("f_p8%d" % idx, [128, 8], F32))
                f_q8 = EC3(nc.sbuf_tensor("f_q8%d" % idx, [128, 8], F32))
                f_xi = EC3(nc.sbuf_tensor("f_xi%d" % idx, [128, D], F32))
                f_pg = EC3(nc.psum_tensor("f_pg%d" % idx, [128, 512], F32))
                f_pg2 = EC3(nc.psum_tensor("f_pg2%d" % idx, [128, 512], F32))
                f_pu = EC3(nc.psum_tensor("f_pu%d" % idx, [128, 512], F32))
                f_pu2 = EC3(nc.psum_tensor("f_pu2%d" % idx, [128, 512], F32))
                f_pd = EC3(nc.psum_tensor("f_pd%d" % idx, [128, 512], F32))
                f_pd2 = EC3(nc.psum_tensor("f_pd2%d" % idx, [128, 512], F32))
                f_pr = EC3(nc.psum_tensor("f_pr%d" % idx, [128, 512], F32))
                @SEC("sync")
                def _(sy):
                    for k in range(16):
                        p.dma(sy, "qs", f_ht[:, k * 1024:(k + 1) * 1024],
                              hh.ap()[:, k * 128:(k + 1) * 128], transpose=True)
                htld = p.cnt["qs"]

                @SEC("gpsimd")
                def _(g):
                    p.dma(g, "qg", f_rw[:, :],
                          routw.rearrange("(k p) e -> p k e", p=128))
                rwld = p.cnt["qg"]

                for tt in range(8):
                    @SEC("tensor")
                    def _(t, tt=tt, htld=htld, rwld=rwld):
                        p.wait(t, "qs", htld)
                        p.wait(t, "qg", rwld)
                        p.wait(t, "ve")
                        for k in range(16):
                            mm = t.matmul(f_pr[:, 0:8],
                                          f_ht[:, k * 1024 + tt * 128:k * 1024 + tt * 128 + 128],
                                          f_rw[:, k * 8:(k + 1) * 8],
                                          start=(k == 0), stop=(k == 15))
                        p.tick(mm, "te")
                    tev = p.cnt["te"]

                    @SEC("vector")
                    def _(v, tt=tt, tev=tev):
                        p.wait(v, "te", tev)
                        p.tick(v.reduce_max(tp1a[:, :], f_pr[:, 0:8], axis=AX.X), "ve")
                        v.drain()
                        p.tick(v.tensor_scalar_mul(tp1a[:, :], tp1a[:, :], -1.0), "ve")
                    mxd = p.cnt["ve"]

                    @SEC("scalar")
                    def _(s, tt=tt, mxd=mxd):
                        p.wait(s, "ve", mxd)
                        p.tick(s.activation(f_p8[:, :], f_pr[:, 0:8], AF.Exp,
                                            bias=tp1a[:, :]), "se")
                    exd = p.cnt["se"]

                    @SEC("vector")
                    def _(v, tt=tt, exd=exd):
                        p.wait(v, "se", exd)
                        p.tick(v.reduce_sum(tp1a[:, :], f_p8[:, :], axis=AX.X), "ve")
                        v.drain()
                        p.tick(v.reciprocal(tp1b[:, :], tp1a[:, :]), "ve")
                        v.drain()
                        p.tick(v.tensor_scalar_mul(f_p8[:, :], f_p8[:, :],
                                                   tp1b[:, :]), "ve")
                        v.drain()
                        p.tick(v.reduce_max(tp1a[:, :], f_p8[:, :], axis=AX.X), "ve")
                        v.drain()
                        p.tick(v.tensor_single_scalar(f_q8[:, :], f_p8[:, :],
                                                      tp1a[:, :], ALU.is_lt), "ve")
                        v.drain()
                        p.tick(v.tensor_mul(f_q8[:, :], f_q8[:, :], f_p8[:, :]), "ve")
                        v.drain()
                        p.tick(v.reduce_max(tp1b[:, :], f_q8[:, :], axis=AX.X), "ve")
                        v.drain()
                        p.tick(v.tensor_add(tp1a[:, :], tp1a[:, :], tp1b[:, :]), "ve")
                        v.drain()
                        p.tick(v.reciprocal(tp1a[:, :], tp1a[:, :]), "ve")
                        p.tick(v.tensor_single_scalar(f_q8[:, :], f_p8[:, :],
                                                      tp1b[:, :], ALU.is_ge), "ve")
                        v.drain()
                        p.tick(v.scalar_tensor_tensor(wtok[:, tt * 8:(tt + 1) * 8],
                                                      f_p8[:, :], tp1a[:, :],
                                                      f_q8[:, :], ALU.mult,
                                                      ALU.mult), "ve")
                rt_done = p.cnt["ve"]

                gu_mark = {}
                act_mark = {}
                wd_mark = {}
                gidx = 0
                for es in range(-1, E):
                    @SEC("gpsimd")
                    def _(g, es=es):
                        if es - 1 in wd_mark:
                            p.wait(g, "te", wd_mark[es - 1])
                        if es < 0:
                            p.dma(g, "qg", f_wd[:, :],
                                  swd.rearrange("(k p) c -> p k c", p=128))
                        else:
                            p.dma(g, "qg", f_wd[:, :],
                                  wdown[es].rearrange("(k p) c -> p k c", p=128))
                    wdld = p.cnt["qg"]

                    for fch in range(4):
                        gu0 = (fch % 2) * 8192

                        @SEC("gpsimd")
                        def _(g, es=es, fch=fch, gu0=gu0):
                            key = (es, fch - 2) if fch >= 2 else (es - 1, fch + 2)
                            if es < 0 and fch < 2:
                                pass
                            elif key in gu_mark:
                                p.wait(g, "te", gu_mark[key])
                            c0, c1 = fch * 256, (fch + 1) * 256
                            if es < 0:
                                gs = swg[:, c0:c1]
                                us = swu[:, c0:c1]
                            else:
                                gs = wgate[es, :, c0:c1]
                                us = wup[es, :, c0:c1]
                            p.dma(g, "qg", f_gu[:, gu0:gu0 + 4096]
                                  .rearrange("p (k c) -> p k c", c=256),
                                  gs.rearrange("(k p) c -> p k c", p=128))
                            p.dma(g, "qg", f_gu[:, gu0 + 4096:gu0 + 8192]
                                  .rearrange("p (k c) -> p k c", c=256),
                                  us.rearrange("(k p) c -> p k c", p=128))
                        guld = p.cnt["qg"]

                        for mf in range(2):
                            fglob = fch * 2 + mf
                            for tch in range(2):
                                par = gidx % 2
                                pg = f_pg if par == 0 else f_pg2
                                pu = f_pu if par == 0 else f_pu2
                                sil = f_sil[:, par * 512:(par + 1) * 512]

                                @SEC("tensor")
                                def _(t, gu0=gu0, mf=mf, tch=tch, guld=guld,
                                      htld=htld, gidx=gidx, pg=pg, pu=pu):
                                    p.wait(t, "qg", guld)
                                    p.wait(t, "qs", htld)
                                    if gidx - 2 in act_mark:
                                        p.wait(t, "ve", act_mark[gidx - 2])
                                    if gidx == 0:
                                        p.wait(t, "ve", rt_done)
                                    off = gu0 + mf * 128
                                    for k in range(16):
                                        t.matmul(pg[:, :],
                                                 f_gu[:, off + k * 256:off + k * 256 + 128],
                                                 f_ht[:, k * 1024 + tch * 512:k * 1024 + (tch + 1) * 512],
                                                 start=(k == 0), stop=(k == 15))
                                    offu = gu0 + 4096 + mf * 128
                                    for k in range(16):
                                        mm = t.matmul(pu[:, :],
                                                      f_gu[:, offu + k * 256:offu + k * 256 + 128],
                                                      f_ht[:, k * 1024 + tch * 512:k * 1024 + (tch + 1) * 512],
                                                      start=(k == 0), stop=(k == 15))
                                    p.tick(mm, "te")
                                tev = p.cnt["te"]
                                gu_mark[(es, fch)] = tev

                                @SEC("scalar")
                                def _(s, tev=tev, gidx=gidx, pg=pg, sil=sil):
                                    p.wait(s, "te", tev)
                                    if gidx - 2 in act_mark:
                                        p.wait(s, "ve", act_mark[gidx - 2])
                                    p.tick(s.activation(sil, pg[:, :], AF.Silu), "se")
                                sev = p.cnt["se"]

                                @SEC("vector")
                                def _(v, es=es, fglob=fglob, tch=tch, sev=sev,
                                      pu=pu, sil=sil):
                                    p.wait(v, "se", sev)
                                    if es - 1 in wd_mark:
                                        p.wait(v, "te", wd_mark[es - 1])
                                    p.tick(v.tensor_mul(
                                        f_act[:, fglob * 1024 + tch * 512:
                                              fglob * 1024 + (tch + 1) * 512],
                                        sil, pu[:, :]), "ve")
                                act_mark[gidx] = p.cnt["ve"]
                                gidx += 1
                    act_done = p.cnt["ve"]

                    dn_mark = {}
                    didx = 0
                    for tch in range(2):
                        for tsub in range(4):
                            gt = tch * 4 + tsub
                            for dm in range(4):
                                par = didx % 2
                                pd = f_pd if par == 0 else f_pd2

                                @SEC("tensor")
                                def _(t, es=es, gt=gt, tch=tch, tsub=tsub, dm=dm,
                                      wdld=wdld, act_done=act_done, didx=didx,
                                      pd=pd):
                                    p.wait(t, "qg", wdld)
                                    p.wait(t, "ve", act_done)
                                    if didx - 2 in dn_mark:
                                        p.wait(t, "ve", dn_mark[didx - 2])
                                    for k in range(8):
                                        mm = t.matmul(
                                            pd[:, :],
                                            f_act[:, k * 1024 + tch * 512 + tsub * 128:
                                                  k * 1024 + tch * 512 + (tsub + 1) * 128],
                                            f_wd[:, k * 2048 + dm * 512:
                                                 k * 2048 + (dm + 1) * 512],
                                            start=(k == 0), stop=(k == 7))
                                    p.tick(mm, "te")
                                tev = p.cnt["te"]
                                if didx == 31:
                                    wd_mark[es] = tev

                                @SEC("vector")
                                def _(v, es=es, gt=gt, dm=dm, tev=tev,
                                      rt_done=rt_done, pd=pd):
                                    p.wait(v, "te", tev)
                                    dst = f_moe[:, gt * 2048 + dm * 512:
                                                gt * 2048 + (dm + 1) * 512]
                                    if es < 0:
                                        p.tick(v.tensor_copy(dst, pd[:, :]), "ve")
                                    else:
                                        p.tick(v.scalar_tensor_tensor(
                                            dst, pd[:, :],
                                            wtok[:, gt * 8 + es:gt * 8 + es + 1],
                                            dst, ALU.mult, ALU.add), "ve")
                                dn_mark[didx] = p.cnt["ve"]
                                didx += 1

                for tt in range(8):
                    @SEC("gpsimd")
                    def _(g, tt=tt):
                        p.wait(g, "ve")
                        p.wait(g, "qs")
                        p.dma(g, "qg", f_xi[:, :], x_in[tt * 128:(tt + 1) * 128, :])
                    xild = p.cnt["qg"]

                    @SEC("vector")
                    def _(v, tt=tt, xild=xild):
                        p.wait(v, "qg", xild)
                        p.tick(v.tensor_add(f_xi[:, :], f_xi[:, :],
                                            f_moe[:, tt * 2048:(tt + 1) * 2048]), "ve")
                    vad = p.cnt["ve"]

                    @SEC("sync")
                    def _(sy, tt=tt, vad=vad):
                        p.wait(sy, "ve", vad)
                        p.dma(sy, "qs", x_out[tt * 128:(tt + 1) * 128, :], f_xi[:, :])
                join_all()

        emit_ffn(x1h.ap(), x2h.ap(), 0)

        with (
            nc.sbuf_tensor("a_x", [128, D], F32) as a_x,
            nc.sbuf_tensor("a_s", [128, D], BF16) as a_s,
        ):
            emit_norm(x2h.ap(), None, None, u2_sl.ap(), a_x[:, :], a_s[:, :], False)
            emit_ag(u2_sl, u2_full)
            join_all()

        ASTK = contextlib.ExitStack()
        with ASTK:
            EC4 = ASTK.enter_context
            a_w = EC4(nc.sbuf_tensor("a_w", [128, 3 * 16 * 256], BF16))
            a_dq1 = EC4(nc.sbuf_tensor("a_dq1", [128, 256], F32))
            a_dq2 = EC4(nc.sbuf_tensor("a_dq2", [128, 256], F32))
            a_bqk = EC4(nc.sbuf_tensor("a_bqk", [128, 4], F32))
            a_bvb = EC4(nc.sbuf_tensor("a_bvb", [128, 256], F32))
            a_q = EC4(nc.sbuf_tensor("a_q", [128, 2 * T], BF16))
            a_k = EC4(nc.sbuf_tensor("a_k", [128, 2 * T], BF16))
            a_v = EC4(nc.sbuf_tensor("a_v", [128, 2 * T], BF16))
            a_kc = EC4(nc.sbuf_tensor("a_kc", [128, L], BF16))
            a_ut = EC4(nc.sbuf_tensor("a_ut", [128, 2 * 16 * 512], BF16))
            a_pt = EC4(nc.sbuf_tensor("a_pt", [128, 2 * 512], BF16))
            a_rd = EC4(nc.sbuf_tensor("a_rd", [1, 512], F32))
            a_rdb = EC4(nc.sbuf_tensor("a_rdb", [128, 512], F32))
            a_ao = EC4(nc.sbuf_tensor("a_ao", [128, 512], BF16))
            a_ps1 = EC4(nc.psum_tensor("a_ps1", [128, 512], F32))
            a_ps2 = EC4(nc.psum_tensor("a_ps2", [128, 512], F32))
            a_pv = EC4(nc.psum_tensor("a_pv", [128, 512], F32))
            a_pd = EC4(nc.psum_tensor("a_pd", [128, 512], F32))
            a_pr = EC4(nc.psum_tensor("a_pr", [128, 512], F32))
            @SEC("gpsimd")
            def _(g):
                p.dma(g, "qg", a_bqk[:, :], bqk)
                p.dma(g, "qg", a_bvb[:, :], bvb)
            for mtx in range(3):
                for k in range(16):
                    @SEC("gpsimd")
                    def _(g, mtx=mtx, k=k):
                        p.wait(g, "ve")
                        p.dma(g, "qg", a_dq1[:, :], wqkv[mtx, k * 128:(k + 1) * 128, :])
                        p.dma(g, "qg", a_dq2[:, :], wqkvs[mtx, k * 128:(k + 1) * 128, :])
                    dqld = p.cnt["qg"]

                    @SEC("vector")
                    def _(v, mtx=mtx, k=k, dqld=dqld):
                        p.wait(v, "qg", dqld)
                        p.tick(v.tensor_mul(
                            a_w[:, (mtx * 16 + k) * 256:(mtx * 16 + k + 1) * 256],
                            a_dq1[:, :], a_dq2[:, :]), "ve")
            wqd = p.cnt["ve"]

            @SEC("vector")
            def _(v, wqd=wqd):
                p.tick(v.tensor_scalar_mul(a_bqk[:, 0:2], a_bqk[:, 0:2], S128I), "ve")
            bqd = p.cnt["ve"]

            for nchk in range(16):
                par = nchk % 2
                t0 = nchk * 512

                @SEC("sync")
                def _(sy, par=par, t0=t0):
                    p.wait(sy, "te")
                    for k in range(16):
                        p.dma(sy, "qs",
                              a_ut[:, (par * 16 + k) * 512:(par * 16 + k + 1) * 512],
                              u2_full.ap()[t0:t0 + 512, k * 128:(k + 1) * 128],
                              transpose=True)
                utld = p.cnt["qs"]

                def UT2(k, par=par):
                    return a_ut[:, (par * 16 + k) * 512:(par * 16 + k + 1) * 512]

                for m in range(2):
                    @SEC("tensor")
                    def _(t, m=m, utld=utld, wqd=wqd, bqd=bqd, UT2=UT2):
                        p.wait(t, "qs", utld)
                        p.wait(t, "ve", bqd)
                        p.wait(t, "se")
                        for k in range(16):
                            mm = t.matmul(a_ps1[:, :],
                                          a_w[:, k * 256 + m * 128:k * 256 + m * 128 + 128],
                                          UT2(k), start=(k == 0), stop=(k == 15))
                        p.tick(mm, "te")
                        for k in range(16):
                            mm = t.matmul(a_ps2[:, :],
                                          a_w[:, (16 + k) * 256 + m * 128:(16 + k) * 256 + m * 128 + 128],
                                          UT2(k), start=(k == 0), stop=(k == 15))
                        p.tick(mm, "te")
                    teq = p.cnt["te"] - 1
                    tek = p.cnt["te"]

                    @SEC("scalar")
                    def _(s, m=m, t0=t0, teq=teq, tek=tek):
                        p.wait(s, "te", teq)
                        p.tick(s.activation(a_q[:, m * T + t0:m * T + t0 + 512],
                                            a_ps1[:, :], AF.Identity, scale=S128I,
                                            bias=a_bqk[:, m:m + 1]), "se")
                        p.wait(s, "te", tek)
                        p.tick(s.activation(a_k[:, m * T + t0:m * T + t0 + 512],
                                            a_ps2[:, :], AF.Identity,
                                            bias=a_bqk[:, 2 + m:3 + m]), "se")
                qk_se = p.cnt["se"]

                for tsub in range(4):
                    @SEC("tensor")
                    def _(t, tsub=tsub, utld=utld, UT2=UT2, qk_se=qk_se):
                        p.wait(t, "se", qk_se)
                        p.wait(t, "ve")
                        for k in range(16):
                            mm = t.matmul(a_pv[:, 0:256],
                                          UT2(k)[:, tsub * 128:(tsub + 1) * 128],
                                          a_w[:, (32 + k) * 256:(32 + k + 1) * 256],
                                          start=(k == 0), stop=(k == 15))
                        p.tick(mm, "te")
                    tev = p.cnt["te"]

                    @SEC("vector")
                    def _(v, t0=t0, tsub=tsub, tev=tev):
                        p.wait(v, "te", tev)
                        gt = (t0 + tsub * 128) // 128
                        p.tick(v.tensor_add(a_v[:, gt * 256:(gt + 1) * 256],
                                            a_pv[:, 0:256], a_bvb[:, :]), "ve")
            qkv_done = p.cnt["ve"]

            for b_ in range(B):
                for hl in range(2):
                    @SEC("vector")
                    def _(v, b_=b_, hl=hl, qkv_done=qkv_done):
                        p.wait(v, "ve", qkv_done)
                        p.wait(v, "te")
                        ks = a_k[:, hl * T + b_ * L:hl * T + (b_ + 1) * L]
                        p.tick(v.reduce_sum(tp1a[:, :], ks, axis=AX.X), "ve")
                        v.drain()
                        p.tick(v.tensor_scalar_mul(tp1a[:, :], tp1a[:, :],
                                                   1.0 / L), "ve")
                        v.drain()
                        p.tick(v.tensor_single_scalar(a_kc[:, :], ks, tp1a[:, :],
                                                      ALU.subtract), "ve")
                    kcd = p.cnt["ve"]

                    for qch in range(8):
                        q0 = b_ * L + qch * 512
                        se_base = p.cnt["se"]
                        score_te = []
                        pvd_te = []

                        @SEC("tensor")
                        def _(t, b_=b_, hl=hl, qch=qch, q0=q0, kcd=kcd,
                              se_base=se_base):
                            p.wait(t, "ve")
                            qsl = a_q[:, hl * T + q0:hl * T + q0 + 512]
                            for kt in range(33):
                                if kt < 32:
                                    psb = a_ps1 if kt % 2 == 0 else a_ps2
                                    if kt >= 2:
                                        t.wait_ge(p.sem["se"], se_base + kt - 1)
                                    mm = t.matmul(psb[:, :],
                                                  a_kc[:, kt * 128:(kt + 1) * 128],
                                                  qsl, start=True, stop=True,
                                                  skip_group_check=True)
                                    score_te.append(p.tick(mm, "te"))
                                if kt >= 1:
                                    kp = kt - 1
                                    t.wait_ge(p.sem["se"], se_base + kp + 1)
                                    ptb = a_pt[:, (kp % 2) * 512:(kp % 2 + 1) * 512]
                                    gt = b_ * 32 + kp
                                    t.matmul(a_pv[:, :],
                                             a_v[:, gt * 256 + hl * 128:gt * 256 + hl * 128 + 128],
                                             ptb, start=(kp == 0), stop=(kp == 31),
                                             skip_group_check=True)
                                    mm = t.matmul(a_pd[0:1, :], c_onesb[:, :], ptb,
                                                  start=(kp == 0), stop=(kp == 31),
                                                  skip_group_check=True)
                                    pvd_te.append(p.tick(mm, "te"))

                        @SEC("scalar")
                        def _(s, score_te=score_te, pvd_te=pvd_te):
                            for kt in range(32):
                                s.wait_ge(p.sem["te"], score_te[kt])
                                if kt >= 2:
                                    s.wait_ge(p.sem["te"], pvd_te[kt - 2])
                                psb = a_ps1 if kt % 2 == 0 else a_ps2
                                p.tick(s.activation(
                                    a_pt[:, (kt % 2) * 512:(kt % 2 + 1) * 512],
                                    psb[:, :], AF.Exp), "se")
                        att_te = p.cnt["te"]

                        @SEC("vector")
                        def _(v, att_te=att_te):
                            p.wait(v, "te", att_te)
                            p.tick(v.reciprocal(a_rd[:, :], a_pd[0:1, :]), "ve")
                        rdd = p.cnt["ve"]

                        @SEC("tensor")
                        def _(t, rdd=rdd):
                            p.wait(t, "ve", rdd)
                            mm = t.matmul(a_pr[:, :], c_onesf[:, :], a_rd[:, :],
                                          start=True, stop=True)
                            p.tick(mm, "te")
                        ter = p.cnt["te"]

                        @SEC("vector")
                        def _(v, ter=ter):
                            p.wait(v, "te", ter)
                            p.wait(v, "qs")
                            p.tick(v.tensor_copy(a_rdb[:, :], a_pr[:, :]), "ve")
                            v.drain()
                            p.tick(v.tensor_mul(a_ao[:, :], a_pv[:, :],
                                                a_rdb[:, :]), "ve")
                        aod = p.cnt["ve"]

                        @SEC("sync")
                        def _(sy, hl=hl, q0=q0, aod=aod):
                            p.wait(sy, "ve", aod)
                            p.dma(sy, "qs",
                                  ao_sl.ap()[hl * 128:(hl + 1) * 128, q0:q0 + 512],
                                  a_ao[:, :])
            join_all()

        emit_ag(ao_sl, ao_full)

        with (
            nc.sbuf_tensor("w_wo", [128, 16 * 2048], BF16) as w_wo,
            nc.sbuf_tensor("w_dq1", [128, 2048], F32) as w_dq1,
            nc.sbuf_tensor("w_dq2", [128, 2048], F32) as w_dq2,
            nc.sbuf_tensor("w_ac", [128, 16 * 1024], BF16) as w_ac,
            nc.sbuf_tensor("w_bo", [128, D], BF16) as w_bo,
            nc.sbuf_tensor("w_x2", [128, D], F32) as w_x2,
            nc.sbuf_tensor("w_x3", [128, D], F32) as w_x3,
            nc.psum_tensor("w_p", [128, 512], F32) as w_p,
            nc.psum_tensor("w_p2", [128, 512], F32) as w_p2,
        ):
            @SEC("gpsimd")
            def _(g):
                p.dma(g, "qg", w_bo[:, :], bob)
            for k in range(16):
                @SEC("gpsimd")
                def _(g, k=k):
                    p.wait(g, "ve")
                    p.dma(g, "qg", w_dq1[:, :], woc[k * 128:(k + 1) * 128, :])
                    p.dma(g, "qg", w_dq2[:, :], wos[k * 128:(k + 1) * 128, :])
                dqld = p.cnt["qg"]

                @SEC("vector")
                def _(v, k=k, dqld=dqld):
                    p.wait(v, "qg", dqld)
                    p.tick(v.tensor_mul(w_wo[:, k * 2048:(k + 1) * 2048],
                                        w_dq1[:, :], w_dq2[:, :]), "ve")
            wod = p.cnt["ve"]

            @SEC("gpsimd")
            def _(g):
                p.wait(g, "cc")
                wacv = w_ac[:, :].rearrange("p (kk c) -> p kk c", c=1024)
                qg_base = p.cnt["qg"]
                for arm in nc.switch_core_id(engines=[g], n=128):
                    r = arm.logical % NCORE
                    for src in range(NCORE):
                        g.wait_ge(p.sem["qg"], qg_base + src * 16)
                        g.dma_start(
                            out=wacv[:, 2 * src:2 * src + 2, :],
                            in_=ao_full.ap()[src, :, r * TPC:(r + 1) * TPC]
                            .rearrange("(kk pp) t -> pp kk t", pp=128),
                        ).then_inc(p.sem["qg"], 16)
                p.cnt["qg"] += 8 * 16
            acld = p.cnt["qg"]

            for tt in range(8):
                @SEC("gpsimd")
                def _(g, tt=tt):
                    p.wait(g, "ve")
                    p.dma(g, "qg", w_x2[:, :], x2h.ap()[tt * 128:(tt + 1) * 128, :])
                x2ld = p.cnt["qg"]

                for dm in range(4):
                    pb = w_p if dm % 2 == 0 else w_p2

                    @SEC("tensor")
                    def _(t, tt=tt, dm=dm, pb=pb, acld=acld, wod=wod):
                        p.wait(t, "qg", acld)
                        p.wait(t, "ve", wod)
                        p.wait(t, "ve")
                        for k in range(16):
                            mm = t.matmul(pb[:, :],
                                          w_ac[:, k * 1024 + tt * 128:k * 1024 + tt * 128 + 128],
                                          w_wo[:, k * 2048 + dm * 512:k * 2048 + (dm + 1) * 512],
                                          start=(k == 0), stop=(k == 15))
                        p.tick(mm, "te")
                    tev = p.cnt["te"]

                    @SEC("vector")
                    def _(v, tt=tt, dm=dm, pb=pb, tev=tev, x2ld=x2ld):
                        p.wait(v, "te", tev)
                        p.wait(v, "qg", x2ld)
                        if dm == 0:
                            p.wait(v, "qs")
                        p.tick(v.scalar_tensor_tensor(
                            w_x3[:, dm * 512:(dm + 1) * 512], pb[:, :], 1.0,
                            w_x2[:, dm * 512:(dm + 1) * 512], ALU.mult, ALU.add), "ve")
                        v.drain()
                        p.tick(v.tensor_add(w_x3[:, dm * 512:(dm + 1) * 512],
                                            w_x3[:, dm * 512:(dm + 1) * 512],
                                            w_bo[:, dm * 512:(dm + 1) * 512]), "ve")
                vd = p.cnt["ve"]

                @SEC("sync")
                def _(sy, tt=tt, vd=vd):
                    p.wait(sy, "ve", vd)
                    p.dma(sy, "qs", x3h.ap()[tt * 128:(tt + 1) * 128, :], w_x3[:, :])
            join_all()

        emit_ffn(x3h.ap(), out, 1)

        if dbg_outs:
            @SEC("gpsimd")
            def _(g):
                p.wait(g, "qs")
                for name, (dst, src) in dbg_outs.items():
                    p.dma(g, "qg", dst, src.ap())

        @SEC("sync")
        def _(sy):
            p.wait(sy, "qs")
            p.wait(sy, "qg")

    return nc


_NC_CACHE = None


def prep_inputs(L0_attn_a_hidden_states, L0_attn_a_rms_weight, L0_attn_a_in_proj_w,
           L0_attn_a_conv_w, L0_attn_a_conv_b, L0_attn_a_dt_bias, L0_attn_a_A_log,
           L0_attn_a_D, L0_attn_a_norm_w, L0_attn_a_out_proj_w,
           L0_attn_b_wq_codes, L0_attn_b_wq_scales, L0_attn_b_bq,
           L0_attn_b_wk_codes, L0_attn_b_wk_scales, L0_attn_b_bk,
           L0_attn_b_wv_codes, L0_attn_b_wv_scales, L0_attn_b_bv,
           L0_attn_b_wo_codes, L0_attn_b_wo_scales, L0_attn_b_bo,
           L0_ffn_rms_weight, L0_ffn_router_w, L0_ffn_w_gate, L0_ffn_w_up,
           L0_ffn_w_down, L0_ffn_sw_gate, L0_ffn_sw_up, L0_ffn_sw_down):
    f32 = np.float32
    A = lambda x: np.ascontiguousarray(x, dtype=f32)

    x = A(L0_attn_a_hidden_states).reshape(T, D)
    inproj = A(L0_attn_a_in_proj_w)
    convw_full = A(L0_attn_a_conv_w)
    convb_full = A(L0_attn_a_conv_b)
    wop_t = A(L0_attn_a_out_proj_w).T.copy()

    sc_rep = lambda s: np.repeat(A(s), 32, axis=0)
    wq_s, wk_s, wv_s = sc_rep(L0_attn_b_wq_scales), sc_rep(L0_attn_b_wk_scales), \
        sc_rep(L0_attn_b_wv_scales)
    wo_srep = sc_rep(L0_attn_b_wo_scales)

    identf = np.eye(128, dtype=f32)
    import ml_dtypes
    identb = np.eye(128, dtype=f32).astype(ml_dtypes.bfloat16)
    trimg = np.triu(np.ones((128, 128), dtype=f32))

    in_maps = []
    for c in range(NCORE):
        heads = list(range(c * HPC, (c + 1) * HPC))
        zrows = np.concatenate([np.arange(h * 64, h * 64 + 64) for h in heads])
        xrows = 4096 + zrows
        brows = np.arange(8192, 8256)
        crows = np.arange(8256, 8320)
        dtrows = 8320 + np.array(heads)
        rows = np.concatenate([zrows, xrows, brows, crows, dtrows])
        wmam_c = np.ascontiguousarray(inproj[rows, :].T)

        cw = np.zeros((6, 128, 4), dtype=f32)
        cb = np.zeros((6, 128), dtype=f32)
        cw[0:4] = convw_full[zrows].reshape(4, 128, 4)
        cb[0:4] = convb_full[zrows].reshape(4, 128)
        cw[4, 0:64] = convw_full[4096:4160]
        cb[4, 0:64] = convb_full[4096:4160]
        cw[5, 0:64] = convw_full[4160:4224]
        cb[5, 0:64] = convb_full[4160:4224]
        convw_c = np.ascontiguousarray(cw.transpose(1, 0, 2).reshape(128, 24))
        convb_c = np.ascontiguousarray(cb.transpose(1, 0).reshape(128, 6))

        fcols = slice(c * 256, (c + 1) * 256)
        m = {
            "x0": np.ascontiguousarray(x[c * TPC:(c + 1) * TPC]),
            "wmam": wmam_c,
            "convw": convw_c,
            "convb": convb_c,
            "dtbias": A(L0_attn_a_dt_bias)[heads].reshape(8, 1),
            "alog": A(L0_attn_a_A_log)[heads].reshape(8, 1),
            "dbc": np.ascontiguousarray(
                np.broadcast_to(A(L0_attn_a_D)[heads], (128, 8))),
            "rmsw1": np.ascontiguousarray(
                np.broadcast_to(A(L0_attn_a_rms_weight), (128, D))),
            "normw": np.ascontiguousarray(
                np.broadcast_to(A(L0_attn_a_norm_w), (128, DI))),
            "wop": wop_t,
            "ffnw": np.ascontiguousarray(
                np.broadcast_to(A(L0_ffn_rms_weight), (128, D))),
            "routw": A(L0_ffn_router_w),
            "wgate": A(L0_ffn_w_gate),
            "wup": A(L0_ffn_w_up),
            "wdown": A(L0_ffn_w_down),
            "swg": A(L0_ffn_sw_gate),
            "swu": A(L0_ffn_sw_up),
            "swd": A(L0_ffn_sw_down),
            "wqkv": np.ascontiguousarray(np.stack([
                A(L0_attn_b_wq_codes)[:, fcols],
                A(L0_attn_b_wk_codes)[:, fcols],
                A(L0_attn_b_wv_codes)[:, fcols]])),
            "wqkvs": np.ascontiguousarray(np.stack([
                wq_s[:, fcols], wk_s[:, fcols], wv_s[:, fcols]])),
            "bqk": np.ascontiguousarray(np.stack([
                A(L0_attn_b_bq)[fcols][0:128], A(L0_attn_b_bq)[fcols][128:256],
                A(L0_attn_b_bk)[fcols][0:128], A(L0_attn_b_bk)[fcols][128:256]],
                axis=1)),
            "bvb": np.ascontiguousarray(
                np.broadcast_to(A(L0_attn_b_bv)[fcols], (128, 256))),
            "woc": A(L0_attn_b_wo_codes),
            "wos": wo_srep,
            "bob": np.ascontiguousarray(
                np.broadcast_to(A(L0_attn_b_bo), (128, D))),
            "identf": identf,
            "cepsi": np.ascontiguousarray(np.broadcast_to(
                np.array([1e-10, 1e-11, 1e-5, 1.0], dtype=f32), (128, 4))),
            "identb": identb,
            "trim": trimg,
        }
        in_maps.append(m)

    return in_maps


def kernel(**inputs):
    global _NC_CACHE
    if _NC_CACHE is None:
        _NC_CACHE = build_nc()
    nc = _NC_CACHE
    in_maps = prep_inputs(**inputs)
    res = run_bass_kernel_spmd(nc, in_maps, core_ids=list(range(NCORE)))
    global _LAST_RES
    _LAST_RES = res
    o = np.concatenate([res.results[c]["o"] for c in range(NCORE)], axis=0)
    return o.reshape(B, L, D).astype(np.float32)


_LAST_RES = None
